# version 01 (final)
# baseline (speedup 1.0000x reference)
import numpy as np
import ml_dtypes
from contextlib import ExitStack
import concourse.bass as bass
import concourse.mybir as mybir
from concourse.bass_utils import run_bass_kernel_spmd
from concourse.aot_env import allow_ici_all_to_all


"""Shared raw-Bass program-builder framework for the megakernel.

Authoring model: the algorithm is written once, in logical dependency
order, appending per-engine thunks to queues. Cross-engine edges are
explicit (semaphore handle, cumulative count) pairs computed at build
time. Emitted per-engine inside @block.<engine> sections at the end.
"""

F32 = mybir.dt.float32
BF16 = mybir.dt.bfloat16
I32 = mybir.dt.int32
AF = mybir.ActivationFunctionType
ALU = mybir.AluOpType
AX = mybir.AxisListType

ENGINES = ("sync", "tensor", "vector", "scalar", "gpsimd")


class _Op:
    __slots__ = ("fn", "incs")

    def __init__(self, fn):
        self.fn = fn
        self.incs = []

    def run(self, eng):
        ins = self.fn(eng)
        for h, k in self.incs:
            ins = ins.then_inc(h, k)
        return ins


class _Wait:
    __slots__ = ("h", "v")

    def __init__(self, h, v):
        self.h = h
        self.v = v

    def run(self, eng):
        eng.wait_ge(self.h, self.v)


class Prog:
    def __init__(self, nc, ctx):
        self.nc = nc
        self.ctx = ctx
        self.q = {e: [] for e in ENGINES}
        self._sems = {}
        self.last_real = {e: None for e in ENGINES}

    def sbuf(self, name, shape, dtype):
        return self.ctx.enter_context(self.nc.sbuf_tensor(name, shape, dtype))

    def psum(self, name, shape, dtype=F32):
        return self.ctx.enter_context(self.nc.psum_tensor(name, shape, dtype))

    def _sem(self, name):
        if name not in self._sems:
            h = self.ctx.enter_context(self.nc.semaphore(name))
            self._sems[name] = [h, 0]
        return self._sems[name]

    def val(self, name):
        return self._sem(name)[1]

    def op(self, eng, fn, sem=None, k=1):
        o = _Op(fn)
        self.q[eng].append(o)
        self.last_real[eng] = o
        if sem is not None:
            return self.mark(eng, sem, k)
        return None

    def mark(self, eng, sem, k=1):
        o = self.last_real[eng]
        assert o is not None, f"no instruction on {eng} to mark"
        assert not o.incs, "instruction already has a sync update (HW limit: 1)"
        s = self._sem(sem)
        s[1] += k
        o.incs.append((s[0], k))
        return s[1]

    def vfence(self, sem, eng="vector"):
        if not hasattr(self, "_fb"):
            self._fb = self.sbuf("_fencebuf", [128, 2], F32)
        fb = self._fb
        if eng == "vector":
            self.op(eng, lambda e: e.tensor_copy(out=fb[:, 1:2], in_=fb[:, 0:1]), sem=sem)
        else:
            self.op(eng, lambda e: e.copy(out=fb[:, 1:2], in_=fb[:, 0:1]), sem=sem)
        return self.val(sem)

    def self_edge(self, eng, sem):
        self.mark(eng, sem)
        self.wait(eng, sem, self.val(sem))

    def wait(self, eng, sem, val=None):
        s = self._sem(sem)
        v = s[1] if val is None else val
        if v > 0:
            self.q[eng].append(_Wait(s[0], v))

    def build(self):
        block = self.ctx.enter_context(self.nc.Block())
        for e in ENGINES:
            ops = self.q[e]
            if not ops:
                continue

            def run_ops(eng, ops=ops):
                for o in ops:
                    o.run(eng)

            getattr(block, e)(run_ops)


def ln_rows(p, name, T):
    d = {k: p.sbuf(name + "_" + k, [1, T], F32) for k in ("mean", "var")}
    d["rstd"] = d["var"]
    d["nmr"] = d["mean"]
    return d


def emit_ln(p, name, x_ap, ln_out_ap, tmp_f32, xsq2, psA, psB, ones_col_bf,
            ones_row_f32, T, nchunks=8, eps=1e-5,
            gamma_pp=None, beta_pp=None, rows=None, eps_ap=None):
    nT = (T + 511) // 512
    sV, sS, sT = name + "_v", name + "_s", name + "_t"
    if rows is None:
        rows = ln_rows(p, name, T)
    mean, var, rstd, nmr = rows["mean"], rows["var"], rows["rstd"], rows["nmr"]
    for c in range(nchunks):
        p.op("vector", lambda e, c=c: e.tensor_copy(out=ln_out_ap(c), in_=x_ap(c)), sem=sV)
        p.wait("scalar", sV, p.val(sV))
        p.op("scalar", lambda e, c=c: e.activation(out=xsq2[c % 2][:, 0:T], in_=ln_out_ap(c),
                                                   func=AF.Square), sem=sS)
        p.wait("tensor", sV, p.val(sV))
        for n in range(nT):
            sl = slice(512 * n, min(T, 512 * (n + 1)))
            p.op("tensor", lambda e, c=c, sl=sl: e.matmul(
                psA[0:1, sl], lhsT=ones_col_bf, rhs=ln_out_ap(c)[:, sl],
                start=(c == 0), stop=(c == nchunks - 1)))
        p.wait("tensor", sS, p.val(sS))
        for n in range(nT):
            sl = slice(512 * n, min(T, 512 * (n + 1)))
            p.op("tensor", lambda e, c=c, sl=sl: e.matmul(
                psB[0:1, sl], lhsT=ones_col_bf, rhs=xsq2[c % 2][:, sl],
                start=(c == 0), stop=(c == nchunks - 1)))
        p.mark("tensor", sT)
        if c < nchunks - 1:
            lag = 0 if xsq2[0] is xsq2[1] else 1
            if p.val(sT) - lag > 0:
                p.wait("scalar", sT, p.val(sT) - lag)
    p.wait("vector", sT, p.val(sT))
    inv_d = 1.0 / (128 * nchunks)
    p.op("vector", lambda e: e.tensor_scalar_mul(mean[:], psA[0:1, 0:T], inv_d))
    p.op("vector", lambda e: e.tensor_scalar_mul(var[:], psB[0:1, 0:T], inv_d))
    p.op("vector", lambda e: e.tensor_mul(tmp_f32[0:1, 0:T], mean[:], mean[:]))
    p.op("vector", lambda e: e.tensor_sub(var[:], var[:], tmp_f32[0:1, 0:T]), sem=sV)
    p.wait("scalar", sV, p.val(sV))
    p.op("scalar", lambda e: e.activation(out=rstd[:], in_=var[:], func=AF.Ln,
                                          bias=(eps_ap if eps_ap is not None else float(eps))))
    p.op("scalar", lambda e: e.activation(out=rstd[:], in_=rstd[:], func=AF.Exp, scale=-0.5), sem=sS)
    p.wait("vector", sS, p.val(sS))
    p.op("vector", lambda e: e.scalar_tensor_tensor(
        out=nmr[:], in0=mean[:], scalar=-1.0, in1=rstd[:], op0=ALU.mult, op1=ALU.mult), sem=sV)
    p.wait("tensor", sV, p.val(sV))
    for n in range(nT):
        sl = slice(512 * n, min(T, 512 * (n + 1)))
        p.op("tensor", lambda e, sl=sl: e.matmul(
            psA[:, sl], lhsT=ones_row_f32, rhs=rstd[0:1, sl], start=True, stop=True))
        p.op("tensor", lambda e, sl=sl: e.matmul(
            psB[:, sl], lhsT=ones_row_f32, rhs=nmr[0:1, sl], start=True, stop=True))
    p.mark("tensor", name + "_t2")
    p.wait("vector", name + "_t2", p.val(name + "_t2"))
    for c in range(nchunks):
        p.op("vector", lambda e, c=c: e.tensor_mul(tmp_f32[:, 0:T], x_ap(c), psA[:, 0:T]))
        if gamma_pp is None:
            p.op("vector", lambda e, c=c: e.tensor_add(ln_out_ap(c), tmp_f32[:, 0:T], psB[:, 0:T]),
                 sem=name + "_done")
        else:
            p.op("vector", lambda e, c=c: e.tensor_add(tmp_f32[:, 0:T], tmp_f32[:, 0:T], psB[:, 0:T]))
            p.op("vector", lambda e, c=c: e.tensor_scalar(
                out=ln_out_ap(c), in0=tmp_f32[:, 0:T], scalar1=gamma_pp(c), scalar2=beta_pp(c),
                op0=ALU.mult, op1=ALU.add), sem=name + "_done")


def np_bf16(x):
    return np.asarray(x).astype(ml_dtypes.bfloat16)



T = 2048


def emit_l0attn(p, xin, xout, Mq, Mk, Mv, onec, oner, cvec, ag_in, ag_out):
    xh = p.sbuf("a_xh", [128, 8 * T], F32)
    ln = p.sbuf("a_ln", [128, 8 * T], BF16)
    tmp = p.sbuf("a_tmp", [128, T], F32)
    _xsq = p.sbuf("a_xsq0", [128, T], BF16)
    xsq2 = [_xsq, _xsq]
    onec_s = p.sbuf("a_onec", [128, 1], BF16)
    oner_s = p.sbuf("a_oner", [1, 128], F32)
    cvec_s = p.sbuf("a_cvec", [1, 8], F32)
    wf = p.sbuf("a_wf", [128, 2 * 1024], F32)
    wb = p.sbuf("a_wb", [128, 8 * 1024], BF16)
    qn = p.sbuf("a_qn", [128, 4 * T], BF16)
    qd = p.sbuf("a_qd", [128, 5 * T], BF16)
    kn = p.sbuf("a_kn", [128, 4 * T], BF16)
    kd = None
    vst = p.sbuf("a_vst", [128, 1024], BF16)
    vv = p.sbuf("a_vv", [128, 2 * 8 * 128], BF16)
    zacc = p.sbuf("a_zacc", [128, 8], F32)
    z1 = p.sbuf("a_z1", [128, 1], F32)
    z2 = p.sbuf("a_z2", [128, 1], F32)
    psA = p.psum("a_psA", [128, T], F32)
    psP = p.psum("a_psP", [128, T], F32)
    rows = ln_rows(p, "a_rows", T)

    xin3 = xin.rearrange("(c p) t -> p c t", p=128)
    xout3 = xout.rearrange("(c p) t -> p c t", p=128)
    agi3 = ag_in.rearrange("(i t) d -> t i d", t=128)
    agv = ag_out.rearrange("(r t) d -> t r d", t=T)

    p.op("sync", lambda e: e.dma_start(out=onec_s[:], in_=onec), sem="ld1", k=16)
    p.op("sync", lambda e: e.dma_start(out=oner_s[:], in_=oner), sem="ld1", k=16)
    p.op("sync", lambda e: e.dma_start(out=cvec_s[:], in_=cvec), sem="ld1", k=16)
    p.op("sync", lambda e: e.dma_start(out=xh[:].rearrange("p (c t) -> p c t", t=T),
                                       in_=xin3), sem="ld1", k=16)
    p.wait("vector", "ld1", 64)
    emit_ln(p, "a_ln1", lambda c: xh[:, c * T:(c + 1) * T],
            lambda c: ln[:, c * T:(c + 1) * T], tmp, xsq2, psA, psP,
            onec_s[:], oner_s[:], T, nchunks=8, rows=rows, eps_ap=cvec_s[0:1, 0:1])

    def load_w(tag, src, ncols, woff, prev=None):
        if prev is not None:
            p.wait("sync", prev, p.val(prev))
        for k in range(8):
            par = k % 2
            if k >= 2:
                p.wait("sync", tag + "_cw", k - 1)
            p.op("sync", lambda e, k=k: e.dma_start(
                out=wf[:, (k % 2) * 1024:(k % 2) * 1024 + ncols],
                in_=src[128 * k:128 * (k + 1), :]), sem=f"{tag}_lw{par}", k=16)
            p.wait("vector", f"{tag}_lw{par}", 16 * (k // 2 + 1))
            p.op("vector", lambda e, k=k: e.tensor_copy(
                out=wb[:, woff + k * ncols:woff + (k + 1) * ncols],
                in_=wf[:, (k % 2) * 1024:(k % 2) * 1024 + ncols]), sem=tag + "_cw")

    load_w("aq", Mq, 512, 0)
    load_w("ak", Mk, 512, 8 * 512, prev="aq_cw")

    p.wait("tensor", "aq_cw", 8)
    p.wait("tensor", "ak_cw", 8)
    p.wait("tensor", "a_ln1_done", p.val("a_ln1_done"))
    for (woff, dst, tagm) in ((0, qn, "mq"), (8 * 512, kn, "mk")):
        for f in range(4):
            if f >= 1 or tagm == "mk":
                p.wait("tensor", "mq_ev" if (tagm == "mk" and f == 0) else tagm + "_ev",
                       4 if (tagm == "mk" and f == 0) else p.val(tagm + "_ev"))
            for k in range(8):
                for n in range(4):
                    nsl = slice(512 * n, 512 * (n + 1))
                    p.op("tensor", lambda e, woff=woff, f=f, k=k, nsl=nsl: e.matmul(
                        psP[:, nsl], lhsT=wb[:, woff + k * 512 + 128 * f:woff + k * 512 + 128 * (f + 1)],
                        rhs=ln[:, k * T + nsl.start:k * T + nsl.stop],
                        start=(k == 0), stop=(k == 7)))
            p.mark("tensor", tagm + "_mm")
            p.wait("vector", tagm + "_mm", p.val(tagm + "_mm"))
            p.op("vector", lambda e, dst=dst, f=f: e.tensor_copy(
                out=dst[:, f * T:(f + 1) * T], in_=psP[:]), sem=tagm + "_ev")

    p.wait("vector", "mq_mm", 4)
    p.wait("vector", "mk_mm", 4)
    load_w("av", Mv, 1024, 0, prev="ak_cw")
    p.wait("tensor", "av_cw", 8)
    for t in range(16):
        if t == 0:
            p.wait("tensor", "mk_ev", 4)
        if t >= 1:
            p.wait("tensor", "mv_ev", t)
        for n in range(2):
            nsl = slice(512 * n, 512 * (n + 1))
            for k in range(8):
                p.op("tensor", lambda e, t=t, k=k, nsl=nsl: e.matmul(
                    psP[:, nsl], lhsT=ln[:, k * T + 128 * t:k * T + 128 * (t + 1)],
                    rhs=wb[:, k * 1024 + nsl.start:k * 1024 + nsl.stop],
                    start=(k == 0), stop=(k == 7)))
        p.mark("tensor", "mv_mm")
        p.wait("vector", "mv_mm", t + 1)
        if t >= 1:
            p.wait("vector", "mv_st", 16 * t)
        p.op("vector", lambda e, t=t: e.tensor_copy(
            out=vst[:, 0:1024], in_=psP[:, 0:1024]), sem="mv_ev")
        p.wait("sync", "mv_ev", t + 1)
        p.op("sync", lambda e, t=t: e.dma_start(
            out=agi3[:, t, :], in_=vst[:, 0:1024]), sem="mv_st", k=16)
    p.wait("sync", "mq_ev", 4)
    p.wait("sync", "mk_ev", 4)
    for (nat, dup, tagd) in ((qn, qd, "dq"),):
        for g in range(4):
            p.op("sync", lambda e, nat=nat, dup=dup, g=g: e.dma_start(
                out=dup[0:64, g * T:(g + 1) * T], in_=nat[64:128, g * T:(g + 1) * T]),
                sem=tagd, k=16)
            if g < 3:
                p.op("sync", lambda e, nat=nat, dup=dup, g=g: e.dma_start(
                    out=dup[64:128, g * T:(g + 1) * T], in_=nat[0:64, (g + 1) * T:(g + 2) * T]),
                    sem=tagd, k=16)
        p.op("sync", lambda e, nat=nat, dup=dup: e.dma_start(
            out=dup[64:128, 4 * T:5 * T], in_=nat[0:64, 0:T]), sem=tagd, k=16)

    p.wait("gpsimd", "mv_st", 16 * 16)
    p.op("gpsimd", lambda e: e.collective_compute(
        "AllGather", ALU.bypass, replica_groups=[list(range(8))],
        ins=[ag_in.opt()], outs=[ag_out.opt()]), sem="cc")
    p.wait("sync", "cc", 1)
    p.wait("sync", "dq", 16 * 8)

    def qk_ap(nat, dup, j, parity, csl):
        if j % 2 == parity:
            base = 64 * (j % 2)
            return nat[base:base + 64, (j // 2) * T + csl.start:(j // 2) * T + csl.stop]
        if j == 0:
            return dup[64:128, 4 * T + csl.start:4 * T + csl.stop]
        g = (j - 1) // 2
        base = 64 * ((j + 1) % 2)
        return dup[base:base + 64, g * T + csl.start:g * T + csl.stop]

    INV8 = 0.125
    NTILES = 128
    for ct in range(NTILES):
        jc, tc = ct // 16, ct % 16
        par = jc % 2
        vb = ct % 2
        if ct >= 2:
            p.wait("sync", "pvmm", (ct - 1) * 8)
        p.op("sync", lambda e, jc=jc, tc=tc, vb=vb: e.dma_start(
            out=vv[:, vb * 1024:(vb + 1) * 1024].rearrange("p (r d) -> p r d", d=128),
            in_=agv[128 * tc:128 * (tc + 1), :, 128 * jc:128 * (jc + 1)]), sem=f"vv_ld{vb}", k=16)
        p.wait("tensor", "a_ln1_done", p.val("a_ln1_done"))
        if ct == 0:
            p.wait("tensor", "mv_ev", 16)
        for jq in range(8):
            if ct > 0 or jq > 0:
                p.wait("tensor", "exp_done", p.val("exp_done"))
            ksl = slice(128 * tc, 128 * (tc + 1))
            for n in range(4):
                nsl = slice(512 * n, 512 * (n + 1))
                p.op("tensor", lambda e, jc=jc, jq=jq, par=par, ksl=ksl, nsl=nsl: e.matmul(
                    psA[:, nsl], lhsT=qk_ap(kn, kd, jc, par, ksl),
                    rhs=qk_ap(qn, qd, jq, par, nsl),
                    start=True, stop=True))
            p.mark("tensor", "smm")
            p.wait("scalar", "smm", p.val("smm"))
            if ct >= 1:
                p.wait("scalar", "pvmm", (ct - 1) * 8 + jq + 1)
            p.op("scalar", lambda e, jq=jq: e.activation(
                out=ln[:, jq * T:(jq + 1) * T], in_=psA[:], func=AF.Exp, scale=INV8,
                accum_out=zacc[:, jq:jq + 1]), sem="exp_done")
        p.wait("vector", "exp_done", p.val("exp_done"))
        p.op("vector", lambda e: e.reduce_sum(z1[:], zacc[:], axis=AX.X))
        p.self_edge("vector", "zse")
        p.op("vector", lambda e: e.reciprocal(z2[:], z1[:]))
        p.self_edge("vector", "zse")
        p.wait("vector", f"vv_ld{vb}", 16 * (ct // 2 + 1))
        p.op("vector", lambda e, vb=vb: e.tensor_scalar_mul(
            vv[:, vb * 1024:(vb + 1) * 1024], vv[:, vb * 1024:(vb + 1) * 1024], z2[:]),
            sem="vv_scaled")
        p.wait("tensor", "vv_scaled", p.val("vv_scaled"))
        for jq in range(8):
            if ct > 0 or jq > 0:
                p.wait("tensor", "pv_ev", p.val("pv_ev"))
            for n in range(4):
                nsl = slice(512 * n, 512 * (n + 1))
                p.op("tensor", lambda e, jq=jq, vb=vb, nsl=nsl: e.matmul(
                    psP[:, nsl], lhsT=vv[:, vb * 1024 + 128 * jq:vb * 1024 + 128 * (jq + 1)],
                    rhs=ln[:, jq * T + nsl.start:jq * T + nsl.stop],
                    start=True, stop=True))
            p.mark("tensor", "pvmm", 1)
            p.wait("vector", "pvmm", p.val("pvmm"))
            p.op("vector", lambda e, jq=jq: e.tensor_add(
                xh[:, jq * T:(jq + 1) * T], xh[:, jq * T:(jq + 1) * T], psP[:]),
                sem="pv_ev")
    import os
    if False:
        dbg_q = p.nc.dram_tensor("dbg_q", [128, 4 * T], BF16, kind="ExternalOutput").ap()
        dbg_k = p.nc.dram_tensor("dbg_k", [128, 4 * T], BF16, kind="ExternalOutput").ap()
        dbg_e = p.nc.dram_tensor("dbg_e", [128, 8 * T], BF16, kind="ExternalOutput").ap()
        dbg_z = p.nc.dram_tensor("dbg_z", [128, 8], F32, kind="ExternalOutput").ap()
        dbg_v = p.nc.dram_tensor("dbg_v", [128, 1024], BF16, kind="ExternalOutput").ap()
        p.wait("sync", "pv_ev", p.val("pv_ev"))
        p.op("sync", lambda e: e.dma_start(out=dbg_q, in_=qn[:]), sem="st", k=16)
        p.op("sync", lambda e: e.dma_start(out=dbg_k, in_=kn[:]), sem="st", k=16)
        p.op("sync", lambda e: e.dma_start(out=dbg_e, in_=ln[:]), sem="st", k=16)
        p.op("sync", lambda e: e.dma_start(out=dbg_z, in_=zacc[:]), sem="st", k=16)
        p.op("sync", lambda e: e.dma_start(out=dbg_v, in_=vv[:, 0:1024]), sem="st", k=16)
        dbg_z1 = p.nc.dram_tensor("dbg_z1", [128, 2], F32, kind="ExternalOutput").ap()
        p.op("sync", lambda e: e.dma_start(out=dbg_z1[:, 0:1], in_=z1[:], allow_slow_non_contiguous=True), sem="st", k=16)
        p.op("sync", lambda e: e.dma_start(out=dbg_z1[:, 1:2], in_=z2[:], allow_slow_non_contiguous=True), sem="st", k=16)
    p.wait("sync", "pv_ev", p.val("pv_ev"))
    p.op("sync", lambda e: e.dma_start(out=xout3,
                                       in_=xh[:].rearrange("p (c t) -> p c t", t=T)), sem="st", k=16)
    p.wait("sync", "st", 16)


def build_a():
    nc = bass.Bass("TRN2", target_bir_lowering=False)
    xin = nc.dram_tensor("xin", [1024, T], F32, kind="ExternalInput").ap()
    Mq = nc.dram_tensor("Mq", [1024, 512], F32, kind="ExternalInput").ap()
    Mk = nc.dram_tensor("Mk", [1024, 512], F32, kind="ExternalInput").ap()
    Mv = nc.dram_tensor("Mv", [1024, 1024], F32, kind="ExternalInput").ap()
    onec = nc.dram_tensor("onec", [128, 1], BF16, kind="ExternalInput").ap()
    oner = nc.dram_tensor("oner", [1, 128], F32, kind="ExternalInput").ap()
    cvec = nc.dram_tensor("cvec", [1, 8], F32, kind="ExternalInput").ap()
    xout = nc.dram_tensor("xout", [1024, T], F32, kind="ExternalOutput").ap()
    ag_in = nc.dram_tensor("ag_in", [T, 1024], BF16).ap()
    ag_out = nc.dram_tensor("ag_out", [8 * T, 1024], BF16).ap()
    with ExitStack() as ctx:
        p = Prog(nc, ctx)
        emit_l0attn(p, xin, xout, Mq, Mk, Mv, onec, oner, cvec, ag_in, ag_out)
        p.build()
    return nc




INV127 = 1.0 / 127.0


def build_b():
    nc = bass.Bass("TRN2", target_bir_lowering=False)
    xin = nc.dram_tensor("xin", [1024, 2048], F32, kind="ExternalInput").ap()
    wgT = nc.dram_tensor("wgT", [1024, 3072], I32, kind="ExternalInput").ap()
    wuT = nc.dram_tensor("wuT", [1024, 3072], I32, kind="ExternalInput").ap()
    wdT = nc.dram_tensor("wdT", [3072, 1024], I32, kind="ExternalInput").ap()
    onec = nc.dram_tensor("onec", [128, 1], BF16, kind="ExternalInput").ap()
    cvec = nc.dram_tensor("cvec", [1, 8], F32, kind="ExternalInput").ap()
    oner = nc.dram_tensor("oner", [1, 128], F32, kind="ExternalInput").ap()
    xout = nc.dram_tensor("xout", [1024, 2048], F32, kind="ExternalOutput").ap()

    with ExitStack() as ctx:
        p = Prog(nc, ctx)
        emit_l0ffn(p, xin, xout, wgT, wuT, wdT, onec, oner, cvec)
        p.build()
    return nc


def emit_l0ffn(p, xin, xout, wgT, wuT, wdT, onec, oner, cvec):
    TH = 1024
    xh = p.sbuf("b_xh", [128, 8 * TH], F32)
    ln = p.sbuf("b_ln", [128, 8 * TH], BF16)
    tmp = p.sbuf("b_tmp", [128, TH], F32)
    xsq2 = [p.sbuf(f"b_xsq{i}", [128, TH], BF16) for i in range(2)]
    wbf = p.sbuf("b_wbf", [128, 24576], BF16)
    wi32 = p.sbuf("b_wi32", [128, 2 * 3072], I32)
    onec_s = p.sbuf("b_onec", [128, 1], BF16)
    cvec_s = p.sbuf("b_cvec", [1, 8], F32)
    oner_s = p.sbuf("b_oner", [1, 128], F32)
    gh = p.sbuf("b_gh", [128, 24 * TH], BF16)
    psA = p.psum("b_psA", [128, TH], F32)
    psB = p.psum("b_psB", [128, TH], F32)
    pp = [p.psum(f"b_pp{i}", [128, TH], F32) for i in range(2)]

    rows = ln_rows(p, "b_rows", TH)
    xin3 = xin.rearrange("(c p) t -> p c t", p=128)
    xout3 = xout.rearrange("(c p) t -> p c t", p=128)

    p.op("sync", lambda e: e.dma_start(out=onec_s[:], in_=onec), sem="ld1", k=16)
    p.op("sync", lambda e: e.dma_start(out=oner_s[:], in_=oner), sem="ld1", k=16)
    p.op("sync", lambda e: e.dma_start(out=cvec_s[:], in_=cvec), sem="ld1", k=16)

    def stream_weights(tag, wsrc, nk, width, prev=None):
        sLW, sCW = tag + "_lw", tag + "_cw"
        if prev is not None:
            p.wait("sync", prev, p.val(prev))
        for k in range(nk):
            par = k % 2
            if k >= 2:
                p.wait("sync", sCW, k - 1)
            p.op("sync", lambda e, k=k: e.dma_start(
                out=wi32[:, (k % 2) * width:(k % 2) * width + width],
                in_=wsrc[128 * k:128 * (k + 1), :]), sem=f"{sLW}{par}", k=16)
            p.wait("vector", f"{sLW}{par}", 16 * (k // 2 + 1))
            p.op("vector", lambda e, k=k: e.tensor_copy(
                out=wbf[:, k * width:(k + 1) * width],
                in_=wi32[:, (k % 2) * width:(k % 2) * width + width]), sem=sCW)
        return sCW

    prevcast = None
    for h in range(2):
        sl = slice(TH * h, TH * (h + 1))
        tg = f"h{h}"
        if h == 1:
            p.wait("sync", "st", p.val("st"))
        p.op("sync", lambda e, sl=sl: e.dma_start(out=xh[:].rearrange("p (c t) -> p c t", t=TH),
                                                  in_=xin3[:, :, sl]), sem="ldx", k=16)
        p.wait("vector", "ldx", p.val("ldx"))
        emit_ln(p, f"b_ln{h}", lambda c: xh[:, c * TH:(c + 1) * TH],
                lambda c: ln[:, c * TH:(c + 1) * TH], tmp, xsq2, psA, psB,
                onec_s[:], oner_s[:], TH, nchunks=8, rows=rows,
                eps_ap=cvec_s[0:1, 0:1])
        p.wait("vector", "ld1", 48)
        scw = stream_weights(tg + "g", wgT, 8, 3072, prev=prevcast)
        p.wait("tensor", scw, p.val(scw))
        p.wait("tensor", f"b_ln{h}_done", p.val(f"b_ln{h}_done"))
        for m in range(24):
            if m >= 2:
                p.wait("tensor", tg + "_evg", m - 1)
            for k in range(8):
                for n in range(2):
                    nsl = slice(512 * n, 512 * (n + 1))
                    p.op("tensor", lambda e, m=m, k=k, nsl=nsl: e.matmul(
                        pp[m % 2][:, nsl], lhsT=wbf[:, k * 3072 + 128 * m:k * 3072 + 128 * (m + 1)],
                        rhs=ln[:, k * TH + nsl.start:k * TH + nsl.stop],
                        start=(k == 0), stop=(k == 7)))
            p.mark("tensor", tg + "_mmg")
            p.wait("scalar", tg + "_mmg", m + 1)
            p.op("scalar", lambda e, m=m: e.activation(
                out=gh[:, m * TH:(m + 1) * TH], in_=pp[m % 2][:], func=AF.Silu,
                scale=INV127), sem=tg + "_evg")
        p.wait("vector", tg + "_mmg", 24)
        scw = stream_weights(tg + "u", wuT, 8, 3072, prev=tg + "g_cw")
        p.wait("tensor", scw, p.val(scw))
        p.wait("tensor", tg + "_evg", 24)
        for m in range(24):
            if m >= 2:
                p.wait("tensor", tg + "_evu", m - 1)
            for k in range(8):
                for n in range(2):
                    nsl = slice(512 * n, 512 * (n + 1))
                    p.op("tensor", lambda e, m=m, k=k, nsl=nsl: e.matmul(
                        pp[m % 2][:, nsl], lhsT=wbf[:, k * 3072 + 128 * m:k * 3072 + 128 * (m + 1)],
                        rhs=ln[:, k * TH + nsl.start:k * TH + nsl.stop],
                        start=(k == 0), stop=(k == 7)))
            p.mark("tensor", tg + "_mmu")
            p.wait("vector", tg + "_mmu", m + 1)
            p.op("vector", lambda e, m=m: e.scalar_tensor_tensor(
                out=gh[:, m * TH:(m + 1) * TH], in0=pp[m % 2][:], scalar=INV127,
                in1=gh[:, m * TH:(m + 1) * TH], op0=ALU.mult, op1=ALU.mult), sem=tg + "_evu")
        p.wait("vector", tg + "_mmu", 24)
        scw = stream_weights(tg + "d", wdT, 24, 1024, prev=tg + "u_cw")
        p.wait("tensor", scw, p.val(scw))
        p.wait("tensor", tg + "_evu", 24)
        for d in range(8):
            if d >= 2:
                p.wait("tensor", tg + "_evd", d - 1)
            for k in range(24):
                for n in range(2):
                    nsl = slice(512 * n, 512 * (n + 1))
                    p.op("tensor", lambda e, d=d, k=k, nsl=nsl: e.matmul(
                        pp[d % 2][:, nsl], lhsT=wbf[:, k * 1024 + 128 * d:k * 1024 + 128 * (d + 1)],
                        rhs=gh[:, k * TH + nsl.start:k * TH + nsl.stop],
                        start=(k == 0), stop=(k == 23)))
            p.mark("tensor", tg + "_mmd")
            p.wait("vector", tg + "_mmd", d + 1)
            p.op("vector", lambda e, d=d: e.scalar_tensor_tensor(
                out=xh[:, d * TH:(d + 1) * TH], in0=pp[d % 2][:], scalar=INV127,
                in1=xh[:, d * TH:(d + 1) * TH], op0=ALU.mult, op1=ALU.add), sem=tg + "_evd")
        p.wait("sync", tg + "_evd", 8)
        prevcast = tg + "d_cw"
        p.op("sync", lambda e, sl=sl: e.dma_start(
            out=xout3[:, :, sl], in_=xh[:].rearrange("p (c t) -> p c t", t=TH)), sem="st", k=16)
    p.wait("sync", "st", p.val("st"))




ISQ = float(1.0 / np.sqrt(np.float64(128.0)))


def emit_l1attn(p, io):
    xh = p.sbuf("c_xh", [128, 8 * T], F32)
    ln = p.sbuf("c_ln", [128, 8 * T], BF16)
    tmp = p.sbuf("c_tmp", [128, T], F32)
    _xsq = p.sbuf("c_xsq", [128, T], BF16)
    xsq2 = [_xsq, _xsq]
    onec_s = p.sbuf("c_onec", [128, 1], BF16)
    oner_s = p.sbuf("c_oner", [1, 128], F32)
    cvec_s = p.sbuf("c_cvec", [1, 8], F32)
    rows = ln_rows(p, "c_rows", T)
    wf = p.sbuf("c_wf", [128, 2 * 1024], F32)
    wqb = p.sbuf("c_wqb", [128, 1024], BF16)
    wkb = p.sbuf("c_wkb", [128, 1024], BF16)
    wvb = p.sbuf("c_wvb", [128, 1024], BF16)
    wgb = p.sbuf("c_wgb", [128, 8 * 1024], BF16)
    wob = wgb
    dmask_s = p.sbuf("c_dmask", [128, 128], F32)
    gamq_s = p.sbuf("c_gamq", [128, 128], BF16)
    gk_s = p.sbuf("c_gk", [128, 1], F32)
    gCv_s = p.sbuf("c_gCv", [128, 1], F32)
    eps6v_s = p.sbuf("c_eps6v", [128, 1], F32)
    yf = p.sbuf("c_yf", [128, 128], F32)
    identf_s = p.sbuf("c_identf", [128, 128], F32)
    lnch = p.sbuf("c_lnch", [128, 2 * 1024], BF16)
    qTb = p.sbuf("c_qTb", [128, 128], BF16)
    qsb = p.sbuf("c_qsb", [128, 128], BF16)
    kTb = p.sbuf("c_kTb", [128, 128], BF16)
    ktokb = p.sbuf("c_ktokb", [128, 128], BF16)
    vtokb = p.sbuf("c_vtokb", [128, 128], BF16)
    aTb = p.sbuf("c_aTb", [128, 128], BF16)
    ych = p.sbuf("c_ych", [128, 128], F32)
    ytst = p.sbuf("c_ytst", [128, 2 * 128], BF16)
    Sf = p.sbuf("c_Sf", [128, 128], F32)
    Sb = p.sbuf("c_Sb", [128, 128], BF16)
    sqs = p.sbuf("c_sqs", [128, 128], F32)
    m1 = p.sbuf("c_m1", [128, 1], F32)
    ssq = p.sbuf("c_ssq", [128, 1], F32)
    t1 = p.sbuf("c_t1", [128, 1], F32)
    dbg0 = p.sbuf("c_dbg0", [128, 132], F32)
    gst = p.sbuf("c_gst", [128, T], BF16)
    ytb = p.sbuf("c_ytb", [128, T], BF16)
    ztb = p.sbuf("c_ztb", [128, 8 * T], BF16)
    psA = p.psum("c_psA", [128, T], F32)
    psB = p.psum("c_psB", [128, T], F32)
    pq, pk, pkt, pvt = (psA[:, 512 * b:512 * b + 128] for b in range(4))
    pa2, pO, pU, pT = (psB[:, 512 * b:512 * b + 128] for b in range(4))

    xin3 = io["xin"].rearrange("(c p) t -> p c t", p=128)
    xout3 = io["xout"].rearrange("(c p) t -> p c t", p=128)



    for nm, dst in (("onec", onec_s), ("oner", oner_s), ("cvec", cvec_s),
                    ("dmask", dmask_s), ("gk", gk_s), ("gCv", gCv_s), ("eps6v", eps6v_s),
                    ("identf", identf_s)):
        p.op("sync", lambda e, nm=nm, dst=dst: e.dma_start(out=dst[:], in_=io[nm]), sem="ld1", k=16)
    p.op("sync", lambda e: e.dma_start(out=wf[:, 0:128], in_=io["gamq"]), sem="ld1", k=16)
    p.op("sync", lambda e: e.dma_start(out=xh[:].rearrange("p (c t) -> p c t", t=T),
                                       in_=xin3), sem="ld1", k=16)
    p.wait("vector", "ld1", p.val("ld1"))
    p.op("vector", lambda e: e.tensor_copy(out=gamq_s[:], in_=wf[:, 0:128]), sem="gq")
    emit_ln(p, "c_ln3", lambda c: xh[:, c * T:(c + 1) * T],
            lambda c: ln[:, c * T:(c + 1) * T], tmp, xsq2, psA, psB,
            onec_s[:], oner_s[:], T, nchunks=8, rows=rows, eps_ap=cvec_s[0:1, 0:1])
    p.wait("sync", "c_ln3_done", 8)
    for sgi in range(8):
        p.op("sync", lambda e, sgi=sgi: e.dma_start(
            out=io[f"ag1s_in{sgi}"].rearrange("(c p) t -> p c t", p=128),
            in_=ln[:].rearrange("p (c t) -> p c t", t=T)[:, :, 256 * sgi:256 * (sgi + 1)]),
            sem="agst", k=16)
    p.wait("gpsimd", "agst", 16 * 8)
    for sgi in range(8):
        p.op("gpsimd", lambda e, sgi=sgi: e.collective_compute(
            "AllGather", ALU.bypass, replica_groups=[list(range(8))],
            ins=[io[f"ag1s_in{sgi}"].opt()], outs=[io[f"ag1s_out{sgi}"].opt()]), sem="cc")

    def load_w(tag, src, ncols, dstbuf, nk=8, prev=None):
        if prev is not None:
            p.wait("sync", prev, p.val(prev))
        for k in range(nk):
            par = k % 2
            if k >= 2:
                p.wait("sync", tag + "_cw", k - 1)
            p.op("sync", lambda e, k=k: e.dma_start(
                out=wf[:, (k % 2) * 1024:(k % 2) * 1024 + ncols],
                in_=src[128 * k:128 * (k + 1), :]), sem=f"{tag}_lw{par}", k=16)
            p.wait("vector", f"{tag}_lw{par}", 16 * (k // 2 + 1))
            p.op("vector", lambda e, k=k: e.tensor_copy(
                out=dstbuf[:, k * ncols:(k + 1) * ncols],
                in_=wf[:, (k % 2) * 1024:(k % 2) * 1024 + ncols]), sem=tag + "_cw")

    load_w("cq", io["wq"], 128, wqb, prev="gq")
    load_w("ck", io["wk"], 128, wkb, prev="cq_cw")
    load_w("cv", io["wv"], 128, wvb, prev="ck_cw")
    load_w("cg", io["wg"], 1024, wgb, prev="cv_cw")

    p.op("vector", lambda e: e.memset(Sf[:], 0.0), sem="sup")
    p.op("vector", lambda e: e.memset(Sb[:], 0.0))
    p.op("vector", lambda e: e.tensor_copy(out=t1[:], in_=Sb[:, 0:1]), sem="sbf")
    p.wait("sync", "cc", 8)
    p.wait("tensor", "cq_cw", 8)
    p.wait("tensor", "ck_cw", 8)
    p.wait("tensor", "cv_cw", 8)
    p.wait("tensor", "c_ln3_done", 8)

    NCH = 128
    for i in range(NCH):
        db = i % 2
        r, tl = i // 16, 128 * (i % 16)
        if i >= 2:
            p.wait("sync", "prjT", 4 * (i - 1))
        sgi, tof = tl // 256, tl % 256
        p.op("sync", lambda e, db=db, r=r, sgi=sgi, tof=tof: e.dma_start(
            out=lnch[:, db * 1024:(db + 1) * 1024].rearrange("p (c t) -> p c t", t=128),
            in_=io[f"ag1s_out{sgi}"].rearrange("(r c p) t -> p r c t", p=128, c=8)[:, r, :, tof:tof + 128]),
            sem=f"lnld{db}", k=16)
        p.wait("tensor", f"lnld{db}", 16 * (i // 2 + 1))
        if i >= 1:
            p.wait("tensor", "pev", 4 * i)
            p.wait("tensor", "yap", i)
            p.wait("tensor", "sbf", i + 1)
            p.wait("vector", "umm", i)
            p.wait("scalar", "sbf", i + 1)
        for (pt, wsrc, tgt) in ((pq, wqb, "q"), (pk, wkb, "k")):
            for k in range(8):
                p.op("tensor", lambda e, pt=pt, wsrc=wsrc, k=k, db=db: e.matmul(
                    pt, lhsT=wsrc[:, 128 * k:128 * (k + 1)],
                    rhs=lnch[:, db * 1024 + 128 * k:db * 1024 + 128 * (k + 1)],
                    start=(k == 0), stop=(k == 7)))
            p.mark("tensor", "prjT")
        for (pt, wsrc) in ((pkt, wkb), (pvt, wvb)):
            for k in range(8):
                p.op("tensor", lambda e, pt=pt, wsrc=wsrc, k=k, db=db: e.matmul(
                    pt, lhsT=lnch[:, db * 1024 + 128 * k:db * 1024 + 128 * (k + 1)],
                    rhs=wsrc[:, 128 * k:128 * (k + 1)],
                    start=(k == 0), stop=(k == 7)))
            p.mark("tensor", "prjT")
        p.wait("vector", "prjT", 4 * i + 2)
        p.op("vector", lambda e: e.tensor_scalar_mul(qTb[:], pq, ISQ), sem="pev")
        p.op("vector", lambda e: e.tensor_copy(out=kTb[:], in_=pk), sem="pev")
        p.wait("vector", "prjT", 4 * i + 4)
        p.op("vector", lambda e: e.tensor_scalar_mul(ktokb[:], pkt, gk_s[:]), sem="pev")
        p.op("vector", lambda e: e.tensor_copy(out=vtokb[:], in_=pvt), sem="pev")
        p.op("vector", lambda e: e.tensor_mul(qsb[:], qTb[:], gamq_s[:]), sem="vch")
        p.wait("tensor", "vch", p.val("vch"))
        if i >= 1:
            p.wait("tensor", "adc", i)
        p.op("tensor", lambda e: e.matmul(pa2, lhsT=kTb[:], rhs=qTb[:],
                                          start=True, stop=True), sem="amm")
        p.wait("vector", "amm", i + 1)
        p.op("vector", lambda e: e.tensor_mul(aTb[:], pa2, dmask_s[:]), sem="adc")
        p.wait("tensor", "adc", i + 1)
        p.wait("tensor", "sbf", i + 1)
        if i >= 1:
            p.wait("tensor", "ocp", i)
        p.op("tensor", lambda e: e.matmul(pO, lhsT=aTb[:], rhs=vtokb[:],
                                          start=True, stop=False))
        p.op("tensor", lambda e: e.matmul(pO, lhsT=qsb[:], rhs=Sb[:],
                                          start=False, stop=True), sem="omm")
        p.wait("vector", "omm", i + 1)
        if i >= 1:
            p.wait("vector", "yap", i)
        p.op("vector", lambda e: e.tensor_copy(out=yf[:], in_=pO), sem="ocp")
        p.wait("vector", "ocp", p.val("ocp"))
        p.op("vector", lambda e: e.tensor_mul(sqs[:], yf[:], yf[:]))
        p.self_edge("vector", "gse")
        p.op("vector", lambda e: e.reduce_sum(ssq[:], sqs[:], axis=AX.X))
        p.op("vector", lambda e: e.reduce_sum(m1[:], yf[:], axis=AX.X))
        p.self_edge("vector", "gse")
        p.op("vector", lambda e: e.tensor_scalar_mul(m1[:], m1[:], -1.0 / 128.0))
        p.op("vector", lambda e: e.tensor_scalar_mul(ssq[:], ssq[:], 1.0 / 128.0))
        p.op("vector", lambda e: e.tensor_mul(t1[:], m1[:], m1[:]))
        p.op("vector", lambda e: e.tensor_sub(ssq[:], ssq[:], t1[:]))
        p.op("vector", lambda e: e.tensor_scalar_max(ssq[:], ssq[:], 0.0), sem="gnv")
        p.wait("scalar", "gnv", p.val("gnv"))
        p.op("scalar", lambda e: e.activation(out=ssq[:], in_=ssq[:], func=AF.Ln,
                                              bias=eps6v_s[:]))
        p.self_edge("scalar", "sse")
        p.op("scalar", lambda e: e.activation(out=ssq[:], in_=ssq[:], func=AF.Exp,
                                              scale=-0.5))
        p.op("scalar", lambda e: e.copy(out=sqs[:, 0:1], in_=ssq[:]), sem="gns")
        p.wait("vector", "gns", p.val("gns"))
        p.op("vector", lambda e: e.tensor_scalar(
            out=ych[:], in0=yf[:], scalar1=m1[:], scalar2=ssq[:],
            op0=ALU.add, op1=ALU.mult), sem="yap")
        if False:
            p.wait("sync", "ocp", i + 1)
            p.op("sync", lambda e, i=i: e.dma_start(out=dyfall[:, 128 * i:128 * (i + 1)], in_=yf[:]), sem="dby", k=16)
            p.wait("sync", "yap", i + 1)
            p.op("sync", lambda e, i=i: e.dma_start(out=dycha[:, 128 * i:128 * (i + 1)], in_=ych[:]), sem="dby", k=16)
            p.wait("sync", "gnv", p.val("gnv"))
            p.op("sync", lambda e, i=i: e.dma_start(out=dstat[:, 2 * i:2 * i + 1], in_=ssq[:], allow_slow_non_contiguous=True), sem="dby", k=16)
            p.op("sync", lambda e, i=i: e.dma_start(out=dstat[:, 2 * i + 1:2 * i + 2], in_=m1[:], allow_slow_non_contiguous=True), sem="dby", k=16)
        p.wait("tensor", "yap", i + 1)
        if i >= 1:
            p.wait("tensor", "ycp", i)
        p.op("tensor", lambda e: e.transpose(pT, in_=ych[:], identity=identf_s[:]),
             sem="ytr")
        p.wait("vector", "ytr", i + 1)
        if i >= 2:
            p.wait("vector", "yst", 16 * (i - 1))
        p.op("vector", lambda e, db=db: e.tensor_copy(
            out=ytst[:, db * 128:(db + 1) * 128], in_=pT), sem="ycp")
        p.wait("sync", "ycp", i + 1)
        tl2 = 128 * (i % 16)
        p.op("sync", lambda e, i=i, db=db, sg2=tl2 // 256, to2=tl2 % 256: e.dma_start(
            out=io[f"a2as_in{sg2}"][128 * (i // 16):128 * (i // 16) + 128, to2:to2 + 128],
            in_=ytst[:, db * 128:(db + 1) * 128]), sem="yst", k=16)
        p.wait("tensor", "sup", i + 1)
        p.op("tensor", lambda e: e.matmul(pU, lhsT=ktokb[:], rhs=vtokb[:],
                                          start=True, stop=True), sem="umm")
        p.wait("vector", "umm", i + 1)
        p.op("vector", lambda e: e.scalar_tensor_tensor(
            out=Sf[:], in0=Sf[:], scalar=gCv_s[:], in1=pU, op0=ALU.mult, op1=ALU.add),
            sem="sup")
        p.op("vector", lambda e: e.tensor_copy(out=Sb[:], in_=Sf[:]))
        p.op("vector", lambda e: e.tensor_copy(out=t1[:], in_=Sb[:, 0:1]), sem="sbf")

    import os
    if False:
        dq = p.nc.dram_tensor("dq", [128, 128], BF16, kind="ExternalOutput").ap()
        dk_ = p.nc.dram_tensor("dk", [128, 128], BF16, kind="ExternalOutput").ap()
        dkt = p.nc.dram_tensor("dkt", [128, 128], BF16, kind="ExternalOutput").ap()
        dvt = p.nc.dram_tensor("dvt", [128, 128], BF16, kind="ExternalOutput").ap()
        dat = p.nc.dram_tensor("dat", [128, 128], BF16, kind="ExternalOutput").ap()
        dyc = p.nc.dram_tensor("dyc", [128, 128], F32, kind="ExternalOutput").ap()
        dsf = p.nc.dram_tensor("dsf", [128, 128], F32, kind="ExternalOutput").ap()
        dqs = p.nc.dram_tensor("dqs", [128, 128], BF16, kind="ExternalOutput").ap()
        dwk = p.nc.dram_tensor("dwk", [128, 1024], BF16, kind="ExternalOutput").ap()
        dln = p.nc.dram_tensor("dln", [128, 1024], BF16, kind="ExternalOutput").ap()
        p.wait("sync", "sbf", p.val("sbf"))
        p.wait("sync", "yst", p.val("yst"))
        for nm, buf in (("dq", qTb), ("dk", kTb), ("dkt", ktokb), ("dvt", vtokb),
                        ("dat", aTb), ("dyc", ych), ("dsf", Sf), ("dqs", qsb)):
            ap = {"dq": dq, "dk": dk_, "dkt": dkt, "dvt": dvt, "dat": dat,
                  "dyc": dyc, "dsf": dsf, "dqs": dqs}[nm]
            p.op("sync", lambda e, ap=ap, buf=buf: e.dma_start(out=ap, in_=buf[:]), sem="dbst", k=16)
        p.op("sync", lambda e: e.dma_start(out=dwk, in_=wkb[:]), sem="dbst", k=16)
        p.op("sync", lambda e: e.dma_start(out=dln, in_=lnch[:, 1024:2048]), sem="dbst", k=16)
        p.wait("sync", "dbst", p.val("dbst"))
    if False:
        dycha = p.nc.dram_tensor("dycha", [128, 128 * NCH], F32, kind="ExternalOutput").ap()
        dstat = p.nc.dram_tensor("dstat", [128, 2 * NCH], F32, kind="ExternalOutput").ap()
        dyfall = p.nc.dram_tensor("dyfall", [128, 128 * NCH], F32, kind="ExternalOutput").ap()
    p.wait("gpsimd", "yst", 16 * NCH)
    for sgi in range(8):
        p.op("gpsimd", lambda e, sgi=sgi: e.collective_compute(
            "AllToAll", ALU.bypass, replica_groups=[list(range(8))],
            ins=[io[f"a2as_in{sgi}"].opt()], outs=[io[f"a2as_out{sgi}"].opt()]), sem="cc")
    p.wait("sync", "cc", 16)
    p.wait("tensor", "cg_cw", 8)
    p.wait("tensor", "pev", p.val("pev"))

    for fo in range(8):
        if fo >= 1:
            p.wait("tensor", "gev", fo)
        for n in range(4):
            nsl = slice(512 * n, 512 * (n + 1))
            for k in range(8):
                p.op("tensor", lambda e, fo=fo, k=k, nsl=nsl: e.matmul(
                    psA[:, nsl], lhsT=wgb[:, 1024 * k + 128 * fo:1024 * k + 128 * (fo + 1)],
                    rhs=ln[:, k * T + nsl.start:k * T + nsl.stop],
                    start=(k == 0), stop=(k == 7)))
        p.mark("tensor", "gmm")
        p.wait("scalar", "gmm", fo + 1)
        if fo >= 1:
            p.wait("scalar", "zdone", fo)
        p.op("scalar", lambda e: e.activation(out=gst[:], in_=psA[:], func=AF.Silu),
             sem="gev")
        [p.op("sync", lambda e, fo=fo, sgi=sgi: e.dma_start(
            out=ytb[:, 256 * sgi:256 * (sgi + 1)],
            in_=io[f"a2as_out{sgi}"].rearrange("(c p) t -> p c t", p=128)[:, fo, :]),
            sem="ytld", k=16) for sgi in range(8)] if fo == 0 else None
        p.wait("vector", "gev", fo + 1)
        p.wait("vector", "ytld", 16 * 8 * (fo + 1))
        p.op("vector", lambda e, fo=fo: e.tensor_mul(
            ztb[:, fo * T:(fo + 1) * T], gst[:], ytb[:]), sem="zdone")
        if fo < 7:
            p.wait("sync", "zdone", fo + 1)
            for sgi in range(8):
                p.op("sync", lambda e, fo=fo, sgi=sgi: e.dma_start(
                    out=ytb[:, 256 * sgi:256 * (sgi + 1)],
                    in_=io[f"a2as_out{sgi}"].rearrange("(c p) t -> p c t", p=128)[:, fo + 1, :]),
                    sem="ytld", k=16)
    p.wait("vector", "gmm", 8)
    load_w("co", io["wo"], 1024, wob, prev="cg_cw")
    p.wait("tensor", "co_cw", 8)
    for fo in range(8):
        if fo >= 1:
            p.wait("tensor", "oev", fo)
        else:
            p.wait("tensor", "zdone", 8)
        for n in range(4):
            nsl = slice(512 * n, 512 * (n + 1))
            for k in range(8):
                p.op("tensor", lambda e, fo=fo, k=k, nsl=nsl: e.matmul(
                    psB[:, nsl], lhsT=wob[:, 1024 * k + 128 * fo:1024 * k + 128 * (fo + 1)],
                    rhs=ztb[:, k * T + nsl.start:k * T + nsl.stop],
                    start=(k == 0), stop=(k == 7)))
        p.mark("tensor", "omm2")
        p.wait("vector", "omm2", fo + 1)
        p.op("vector", lambda e, fo=fo: e.tensor_add(
            xh[:, fo * T:(fo + 1) * T], xh[:, fo * T:(fo + 1) * T], psB[:]), sem="oev")
    if False:
        da2 = p.nc.dram_tensor("da2", [1024, T], BF16, kind="ExternalOutput").ap()
        dzt = p.nc.dram_tensor("dzt", [128, 8 * T], BF16, kind="ExternalOutput").ap()
        p.op("sync", lambda e: e.dma_start(out=da2, in_=io["a2a_out"]), sem="dbt", k=16)
        p.wait("sync", "zdone", 8)
        p.op("sync", lambda e: e.dma_start(out=dzt, in_=ztb[:]), sem="dbt", k=16)
    dbg0o = p.nc.dram_tensor("dbg0o", [128, 132], F32, kind="ExternalOutput").ap()
    p.op("sync", lambda e: e.dma_start(out=dbg0o, in_=dbg0[:]), sem="dbst0", k=16)
    p.wait("sync", "oev", 8)
    p.op("sync", lambda e: e.dma_start(out=xout3,
                                       in_=xh[:].rearrange("p (c t) -> p c t", t=T)), sem="stx", k=16)
    p.wait("sync", "stx", 16)


def build_c():
    nc = bass.Bass("TRN2", target_bir_lowering=False)
    io = {}
    io["xin"] = nc.dram_tensor("xin", [1024, T], F32, kind="ExternalInput").ap()
    io["wq"] = nc.dram_tensor("wq", [1024, 128], F32, kind="ExternalInput").ap()
    io["wk"] = nc.dram_tensor("wk", [1024, 128], F32, kind="ExternalInput").ap()
    io["wv"] = nc.dram_tensor("wv", [1024, 128], F32, kind="ExternalInput").ap()
    io["wg"] = nc.dram_tensor("wg", [1024, 1024], F32, kind="ExternalInput").ap()
    io["wo"] = nc.dram_tensor("wo", [1024, 1024], F32, kind="ExternalInput").ap()
    io["dmask"] = nc.dram_tensor("dmask", [128, 128], F32, kind="ExternalInput").ap()
    io["gamq"] = nc.dram_tensor("gamq", [128, 128], F32, kind="ExternalInput").ap()
    io["gk"] = nc.dram_tensor("gk", [128, 1], F32, kind="ExternalInput").ap()
    io["gCv"] = nc.dram_tensor("gCv", [128, 1], F32, kind="ExternalInput").ap()
    io["eps6v"] = nc.dram_tensor("eps6v", [128, 1], F32, kind="ExternalInput").ap()
    io["identf"] = nc.dram_tensor("identf", [128, 128], F32, kind="ExternalInput").ap()
    io["onec"] = nc.dram_tensor("onec", [128, 1], BF16, kind="ExternalInput").ap()
    io["oner"] = nc.dram_tensor("oner", [1, 128], F32, kind="ExternalInput").ap()
    io["cvec"] = nc.dram_tensor("cvec", [1, 8], F32, kind="ExternalInput").ap()
    io["xout"] = nc.dram_tensor("xout", [1024, T], F32, kind="ExternalOutput").ap()
    for sgi in range(8):
        io[f"ag1s_in{sgi}"] = nc.dram_tensor(f"ag1s_in{sgi}", [1024, 256], BF16).ap()
        io[f"ag1s_out{sgi}"] = nc.dram_tensor(f"ag1s_out{sgi}", [8 * 1024, 256], BF16).ap()
    for sgi in range(8):
        io[f"a2as_in{sgi}"] = nc.dram_tensor(f"a2as_in{sgi}", [1024, 256], BF16).ap()
        io[f"a2as_out{sgi}"] = nc.dram_tensor(f"a2as_out{sgi}", [1024, 256], BF16).ap()
    io["fence_in"] = nc.dram_tensor("fence_in", [128, 4], F32).ap()
    io["fence_out"] = nc.dram_tensor("fence_out", [128, 4], F32).ap()
    with ExitStack() as ctx, allow_ici_all_to_all():
        p = Prog(nc, ctx)
        emit_l1attn(p, io)
        p.build()
    return nc


def core_inputs_c(z, x2, h):
    H = 8
    g = float(1.0 - np.exp2(np.float64(-5.0 - h)))
    lg = np.log(np.float64(g))
    ti = np.arange(128, dtype=np.float64)
    dmask = np.where(ti[None, :] - ti[:, None] >= 0,
                     np.exp(lg * (ti[None, :] - ti[:, None])), 0.0).astype(np.float32)
    gamq = np.broadcast_to(np.exp(lg * (ti + 1.0)).astype(np.float32), (128, 128)).copy()
    gk = np.exp(lg * (127.0 - ti)).astype(np.float32).reshape(128, 1)
    gCv = np.full((128, 1), np.exp(lg * 128.0), np.float32)
    blk = x2[2048 * h:2048 * (h + 1), :]
    cv = np.zeros((1, 8), np.float32)
    cv[0, 0] = 1e-5
    cv[0, 1] = 1e-6
    return dict(
        xin=np.ascontiguousarray(blk.T),
        wq=np.ascontiguousarray(z["L1_attn_Wq"][:, 128 * h:128 * (h + 1)]),
        wk=np.ascontiguousarray(z["L1_attn_Wk"][:, 128 * h:128 * (h + 1)]),
        wv=np.ascontiguousarray(z["L1_attn_Wv"][:, 128 * h:128 * (h + 1)]),
        wg=z["L1_attn_W_G"], wo=z["L1_attn_W_O"],
        dmask=dmask, gamq=gamq, gk=gk, gCv=gCv,
        eps6v=np.full((128, 1), 1e-6, np.float32),
        identf=np.eye(128, dtype=np.float32),
        onec=np.ones((128, 1), ml_dtypes.bfloat16),
        oner=np.ones((1, 128), np.float32), cvec=cv)




U32 = mybir.dt.uint32
T, TH, NL = 2048, 1024, 6
TP = T + 1
DI, NMT = 2730, 22
LASTR = DI - 128 * (NMT - 1)
NLYR = NL


def build_d():
    nc = bass.Bass("TRN2", target_bir_lowering=False)
    io = {}
    ei = lambda n, s, d: nc.dram_tensor(n, s, d, kind="ExternalInput").ap()
    io["x3"] = ei("x3", [1024, T], F32)
    io["Wqkv"] = ei("Wqkv", [NL, 1024, 3072], F32)
    io["Wo"] = ei("Wo", [NL, 1024, 1024], F32)
    io["W1"] = ei("W1", [NL, 1024, 5460], F32)
    io["W2"] = ei("W2", [NL, 2730, 1024], F32)
    io["qst"] = ei("qst", [NL, 1024], F32)
    io["kst"] = ei("kst", [NL, 1024], F32)
    io["lng"] = ei("lng", [128, 8 * NL], F32)
    io["lnb"] = ei("lnb", [128, 8 * NL], F32)
    io["b1a"] = ei("b1a", [128, NMT * NL], F32)
    io["b1g"] = ei("b1g", [128, NMT * NL], F32)
    io["b2"] = ei("b2", [128, 8 * NL], F32)
    io["tabq"] = ei("tabq", [2304, 2048], F32)
    io["tabk"] = ei("tabk", [2304, 2048], F32)
    io["maskA"] = ei("maskA", [128, 1024], mybir.dt.bfloat16)
    io["mknegv"] = ei("mknegv", [128, 1], F32)
    io["idxprev"] = ei("idxprev", [128, 1], U32)
    io["hz"] = ei("hz", [128, 1], F32)
    io["eps30"] = ei("eps30", [128, 1], F32)
    io["identb"] = ei("identb", [128, 128], mybir.dt.bfloat16)
    io["onec"] = ei("onec", [128, 1], mybir.dt.bfloat16)
    io["oner"] = ei("oner", [1, 128], F32)
    io["cvec"] = ei("cvec", [1, 8], F32)
    io["xout"] = nc.dram_tensor("xout", [1024, T], F32, kind="ExternalOutput").ap()
    io["hA_in0"] = nc.dram_tensor("hA_in0", [128, 8 * 258], mybir.dt.bfloat16).ap()
    io["hA_out0"] = nc.dram_tensor("hA_out0", [8 * 128, 8 * 258], mybir.dt.bfloat16).ap()
    io["hA_in1"] = nc.dram_tensor("hA_in1", [128, 8 * 258], mybir.dt.bfloat16).ap()
    io["hA_out1"] = nc.dram_tensor("hA_out1", [8 * 128, 8 * 258], mybir.dt.bfloat16).ap()
    io["hB_in0"] = nc.dram_tensor("hB_in0", [128, 8], F32).ap()
    io["hB_out0"] = nc.dram_tensor("hB_out0", [8 * 128, 8], F32).ap()
    io["hB_in1"] = nc.dram_tensor("hB_in1", [128, 8], F32).ap()
    io["hB_out1"] = nc.dram_tensor("hB_out1", [8 * 128, 8], F32).ap()
    with ExitStack() as ctx:
        p = Prog(nc, ctx)
        emit_d(p, io)
        p.build()
    return nc


def emit_d(p, io):
    BF = BF16
    xf = p.sbuf("d_xf", [128, 8 * TP], F32)
    arena = p.sbuf("d_ar", [128, 45056], BF)
    wst = p.sbuf("d_wst", [128, 2048], F32)
    wbb = p.sbuf("d_wbb", [128, 2 * 2048], BF)
    ksave = p.sbuf("d_ksave", [128, 2048], BF)
    maskA_s = p.sbuf("d_maskA", [128, 1024], BF)
    mkneg_s = p.sbuf("d_mkneg", [128, 1], F32)
    tmp = p.sbuf("d_tmp", [128, 1024], F32)
    rows = ln_rows(p, "d_rows", TH)
    gbuf = p.sbuf("d_gbuf", [128, TH], BF)
    pexp = p.sbuf("d_pexp", [128, 512], BF)
    ptb = p.sbuf("d_ptb", [128, 512], BF)
    nmx = p.sbuf("d_nmx", [128, 1], F32)
    rsum = p.sbuf("d_rsum", [128, 1], F32)
    invz = p.sbuf("d_invz", [128, 1], F32)
    lsq = p.sbuf("d_lsq", [128, 1024], BF)
    xsq2 = [lsq, lsq]
    simc = tmp[:, 0:512]
    lss = p.sbuf("d_lss", [128, 16], F32)
    qsb = p.sbuf("d_qsb", [128, 2 * 1024], BF)
    tabs = p.sbuf("d_tabs", [128, 2048], F32)
    onec_s = p.sbuf("d_onec", [128, 1], BF)
    oner_s = p.sbuf("d_oner", [1, 128], F32)
    cvec_s = p.sbuf("d_cvec", [1, 8], F32)
    identb_s = p.sbuf("d_identb", [128, 128], BF)
    idx_s = p.sbuf("d_idx", [128, 1], U32)
    hz_s = p.sbuf("d_hz", [128, 1], F32)
    eps30_s = p.sbuf("d_eps30", [128, 1], F32)
    lng_s = p.sbuf("d_lng", [128, 8 * NL], F32)
    lnb_s = p.sbuf("d_lnb", [128, 8 * NL], F32)
    b1a_s = p.sbuf("d_b1a", [128, NMT * NL], F32)
    b1g_s = p.sbuf("d_b1g", [128, NMT * NL], F32)
    b2_s = p.sbuf("d_b2", [128, 8 * NL], F32)
    psA = p.psum("d_psA", [128, 2048], F32)
    psB = p.psum("d_psB", [128, 1536], F32)
    psT = p.psum("d_psT", [128, 1024], BF)

    AR = lambda o, n: arena[:, o:o + n]
    v_tok = AR(0, 10 * 1024)
    qk_tok = AR(10 * 1024, 16 * 1024)
    kT = AR(26 * 1024, 10 * 1024)
    qT = AR(36 * 1024, 8 * 1024)
    aout = AR(10 * 1024, 8 * 1024)
    ln_ff = AR(30 * 1024, 8 * 1024)
    h2 = AR(0, NMT * 1024)
    xs_bf = AR(26 * 1024, 8 * 1024)
    xs2_bf = AR(22 * 1024, 8 * 1024)
    xshal = AR(34 * 1024, 2056)
    xhal = AR(0, 2064)
    khalo = AR(42 * 1024, 2048)

    xin3 = io["x3"].rearrange("(c p) t -> p c t", p=128)
    xout3 = io["xout"].rearrange("(c p) t -> p c t", p=128)
    xf3 = xf[:].rearrange("p (c t) -> p c t", t=TP)
    xsv = xs_bf.rearrange("p (c t) -> p c t", t=TH)
    xs2v = xs2_bf.rearrange("p (c t) -> p c t", t=TH)
    xh3 = xhal.rearrange("p (c t) -> p c t", t=258)
    xs3 = xshal.rearrange("p (c t) -> p c t", t=257)
    qkv3 = qk_tok.rearrange("p (t f) -> p t f", f=2048)
    vv3 = v_tok.rearrange("p (t f) -> p t f", f=1024)
    kh3 = khalo.rearrange("p (t f) -> p t f", f=1024)
    kT3 = kT.rearrange("p (c t) -> p c t", t=1280)
    qT3 = qT.rearrange("p (c t) -> p c t", t=1024)
    ao3 = aout.rearrange("p (c t) -> p c t", t=1024)
    ln3 = ln_ff.rearrange("p (c t) -> p c t", t=1024)
    h23 = h2.rearrange("p (j t) -> p j t", t=1024)

    for nm, dst in (("onec", onec_s), ("oner", oner_s), ("cvec", cvec_s),
                    ("identb", identb_s), ("idxprev", idx_s), ("hz", hz_s),
                    ("eps30", eps30_s), ("maskA", maskA_s), ("mknegv", mkneg_s),
                    ("lng", lng_s), ("lnb", lnb_s), ("b1a", b1a_s),
                    ("b1g", b1g_s), ("b2", b2_s)):
        p.op("sync", lambda e, nm=nm, dst=dst: e.dma_start(out=dst[:], in_=io[nm]),
             sem="ld1", k=16)
    p.op("sync", lambda e: e.dma_start(
        out=xf3[:, :, 1:], in_=xin3), sem="ld1", k=16)
    p.wait("vector", "ld1", p.val("ld1"))
    p.wait("scalar", "ld1", p.val("ld1"))

    for hf in range(2):
        t0 = 1 + TH * hf
        nm = f"ln4h{hf}"
        emit_ln(p, nm, lambda c, t0=t0: xf[:, c * TP + t0:c * TP + t0 + TH],
                lambda c: xs2_bf[:, c * TH:(c + 1) * TH], tmp[:, 0:TH], xsq2,
                psA[:, 0:1024], psB[:, 0:1024], onec_s[:], oner_s[:], TH,
                rows=rows, eps_ap=cvec_s[0:1, 0:1])
        p.wait("vector", nm + "_done", p.val(nm + "_done"))
        for c in range(8):
            p.op("vector", lambda e, c=c, t0=t0: e.tensor_copy(
                out=xf[:, c * TP + t0:c * TP + t0 + TH],
                in_=xs2_bf[:, c * TH:(c + 1) * TH]), sem="ln4c")


    DSTOP = ''
    for li in range(NLYR):
        L = f"L{li}"
        agb = li % 2

        if li == 0:
            p.wait("vector", "ln4c", p.val("ln4c"))
        else:
            p.wait("vector", "ffx", p.val("ffx"))
        for c in range(8):
            p.op("vector", lambda e, c=c: e.tensor_copy(
                out=xh3[:, c, :], in_=xf[:, c * TP + (TP - 258):c * TP + TP]), sem="hc")
        p.wait("sync", "hc", p.val("hc"))
        p.op("sync", lambda e, agb=agb: e.dma_start(out=io[f"hA_in{agb}"], in_=xhal),
             sem="hst", k=16)
        p.wait("gpsimd", "hst", p.val("hst"))
        p.op("gpsimd", lambda e, agb=agb: e.collective_compute(
            "AllGather", ALU.bypass, replica_groups=[list(range(8))],
            ins=[io[f"hA_in{agb}"].opt()], outs=[io[f"hA_out{agb}"].opt()]), sem="ccA")
        p.wait("gpsimd", "ccA", p.val("ccA"))
        p.op("gpsimd", lambda e, agb=agb: e.indirect_dma_start(
            out=xhal, out_offset=None, in_=io[f"hA_out{agb}"],
            in_offset=bass.IndirectOffsetOnAxis(ap=idx_s[:], axis=0)), sem="hgot", k=16)
        p.wait("vector", "hgot", p.val("hgot"))
        p.op("vector", lambda e: e.tensor_scalar_mul(xhal, xhal, hz_s[:]))
        p.op("vector", lambda e: e.tensor_copy(out=xs3[:, 0:4, :], in_=xh3[:, 0:4, 1:258]))
        p.op("vector", lambda e: e.tensor_copy(out=xs3[:, 4:8, :], in_=xh3[:, 4:8, 0:257]))
        p.op("vector", lambda e: e.tensor_copy(out=xf3[:, :, 0:1], in_=xh3[:, :, 257:258]))
        p.vfence("hrdy")

        p.op("sync", lambda e, li=li: e.dma_start(out=tmp[0:1, 0:1024], in_=io["qst"][li:li + 1, :]),
             sem="qsl", k=16)
        p.op("sync", lambda e, li=li: e.dma_start(out=rows["mean"][0:1, 0:1024], in_=io["kst"][li:li + 1, :]),
             sem="qsl", k=16)
        p.wait("tensor", "qsl", p.val("qsl"))
        if li > 0:
            p.wait("tensor", "ffx", p.val("ffx"))
        for n in range(2):
            sl = slice(512 * n, 512 * (n + 1))
            p.op("tensor", lambda e, sl=sl: e.matmul(psA[:, sl], lhsT=oner_s[:],
                                                     rhs=tmp[0:1, sl], start=True, stop=True))
            p.op("tensor", lambda e, sl=sl: e.matmul(psA[:, 1024 + sl.start:1024 + sl.stop],
                                                     lhsT=oner_s[:], rhs=rows["mean"][0:1, sl],
                                                     start=True, stop=True))
        p.mark("tensor", "qsbm")
        p.wait("vector", "qsbm", p.val("qsbm"))
        p.op("vector", lambda e: e.tensor_copy(out=qsb[:, 0:1024], in_=psA[:, 0:1024]))
        p.op("vector", lambda e: e.tensor_copy(out=qsb[:, 1024:2048], in_=psA[:, 1024:2048]),
             sem="qsbv")

        for hf in range(2):
            t0 = 1 + TH * hf
            H = f"{L}h{hf}"
            if hf == 1:
                p.wait("vector", "aov", p.val("aov"))
                p.op("vector", lambda e: e.tensor_copy(
                    out=ksave[:].rearrange("p (c t) -> p c t", t=256),
                    in_=kT3[:, :, 1024:1280]))
                p.op("vector", lambda e: e.tensor_copy(out=vv3[:, 0, :], in_=vv3[:, 8, :]))
                p.op("vector", lambda e: e.tensor_copy(out=vv3[:, 1, :], in_=vv3[:, 9, :]),
                     sem="hrst")

            p.wait("vector", "hrdy", p.val("hrdy"))
            for c in range(8):
                off = t0 if c < 4 else t0 - 1
                p.op("vector", lambda e, c=c, off=off: e.tensor_copy(
                    out=xsv[:, c, :], in_=xf[:, c * TP + off:c * TP + off + TH]), sem="xs")
            p.vfence("xsf")
            pjc = 0
            pjmm_base = p.val("pjmm")
            bwl, bwc, bpje = p.val("wl"), p.val("wcast"), p.val("pje")
            bwcf = p.val("wcf")
            p.wait("sync", "w2c", p.val("w2c"))
            p.wait("sync", "woc", p.val("woc"))
            for n in range(12):
                pp = n % 2
                if n >= 1:
                    p.wait("sync", "wcast", bwc + n)
                p.op("sync", lambda e, n=n, li=li: e.dma_start(
                    out=wst[:].rearrange("p (c w) -> p c w", w=256),
                    in_=io["Wqkv"].rearrange("l (c p) w -> p l c w", p=128)[:, li, :, 256 * n:256 * (n + 1)]),
                    sem="wl", k=16)
                p.wait("vector", "wl", bwl + 16 * (n + 1))
                if n >= 2:
                    hx = 2 if hf == 0 else 0
                    cum = (lambda m: 8 * (m + 1) if m < 4 else 32 + (8 + hx) * (m - 3))
                    p.wait("vector", "pjmm", pjmm_base + cum(n - 2))
                p.op("vector", lambda e, pp=pp: e.tensor_copy(
                    out=wbb[:, pp * 2048:(pp + 1) * 2048], in_=wst[:]), sem="wcast")
                p.vfence("wcf")
                p.wait("tensor", "wcf", bwcf + n + 1)
                p.wait("tensor", "xsf", p.val("xsf"))
                if hf == 0:
                    p.wait("tensor", "hrdy", p.val("hrdy"))
                tl = ([("s", 1), ("s", 129)] if (n >= 4 and hf == 0) else []) + [("o", i) for i in range(8)]
                for tt in tl:
                    kind, ti = tt
                    lhs = (xs3[:, :, ti:ti + 128] if kind == "s"
                           else xsv[:, :, 128 * ti:128 * (ti + 1)])
                    pb = pjc % 2
                    pjc += 1
                    if pjc > 2:
                        p.wait("tensor", "pje", bpje + pjc - 2)
                    for k in range(8):
                        p.op("tensor", lambda e, lhs=lhs, k=k, pp=pp, pb=pb: e.matmul(
                            psA[:, 512 * pb:512 * pb + 256], lhsT=lhs[:, k, :],
                            rhs=wbb[:, pp * 2048 + k * 256:pp * 2048 + (k + 1) * 256],
                            start=(k == 0), stop=(k == 7)))
                    p.mark("tensor", "pjmm")
                    p.wait("vector", "pjmm", p.val("pjmm"))
                    if n < 4:
                        dst = qkv3[:, ti, 256 * n:256 * (n + 1)]
                    elif n < 8:
                        if kind == "s":
                            dst = kh3[:, (ti - 1) // 128, 256 * (n - 4):256 * (n - 4) + 256]
                        else:
                            dst = qkv3[:, ti, 1024 + 256 * (n - 4):1024 + 256 * (n - 4) + 256]
                    else:
                        if kind == "s":
                            dst = vv3[:, (ti - 1) // 128, 256 * (n - 8):256 * (n - 8) + 256]
                        else:
                            dst = vv3[:, 2 + ti, 256 * (n - 8):256 * (n - 8) + 256]
                    p.op("vector", lambda e, dst=dst, pb=pb: e.tensor_copy(
                        out=dst, in_=psA[:, 512 * pb:512 * pb + 256]), sem="pje")

            tiles = ([("kh", i) for i in range(2)] +
                     [("q", i) for i in range(8)] + [("k", i) for i in range(8)])
            trc = 0
            btu, btev = p.val("tu"), p.val("tev")
            for kind, ti in tiles:
                pass
                if kind == "q":
                    src = qkv3[:, ti, 0:1024]
                    tabio, tabrow = io["tabq"], 256 + TH * hf + 128 * ti
                    qoff = 0
                elif kind == "k":
                    src = qkv3[:, ti, 1024:2048]
                    tabio, tabrow = io["tabk"], 256 + TH * hf + 128 * ti
                    qoff = 1024
                else:
                    src = kh3[:, ti, :]
                    tabio, tabrow = io["tabk"], 128 * ti
                    qoff = 1024
                if kind == "kh" and hf == 1:
                    continue
                trc += 1
                p.wait("sync", "tu", p.val("tu"))
                p.op("sync", lambda e, tabio=tabio, tabrow=tabrow: e.dma_start(
                    out=tabs[:], in_=tabio[tabrow:tabrow + 128, :]), sem="tl", k=16)
                p.wait("scalar", "pje", p.val("pje"))
                if trc > 1:
                    p.wait("scalar", "tu", btu + trc - 1)
                p.op("scalar", lambda e, src=src: e.activation(out=lsq[:], in_=src,
                                                               func=AF.Square), sem="rsq1")
                p.wait("vector", "rsq1", p.val("rsq1"))
                p.op("vector", lambda e: e.reduce_sum(
                    lss[:].rearrange("p (h x) -> p h x", x=1),
                    lsq[:].rearrange("p (h d) -> p h d", d=64), axis=AX.X))
                p.self_edge("vector", "rse")
                p.wait("scalar", "rse", p.val("rse"))
                p.op("scalar", lambda e: e.activation(out=lss[:], in_=lss[:], func=AF.Ln,
                                                      bias=eps30_s[:]))
                p.self_edge("scalar", "rsse")
                p.op("scalar", lambda e: e.activation(out=lss[:], in_=lss[:], func=AF.Exp,
                                                      scale=-0.5), sem="rsq2")
                p.wait("vector", "rsq2", p.val("rsq2"))
                p.op("vector", lambda e: e.tensor_scalar_min(lss[:], lss[:], 1.0e12))
                for h in range(16):
                    p.op("vector", lambda e, src=src, h=h: e.tensor_scalar_mul(
                        src[:, 64 * h:64 * (h + 1)], src[:, 64 * h:64 * (h + 1)],
                        lss[:, h:h + 1]))
                p.wait("vector", "qsbv", p.val("qsbv"))
                p.op("vector", lambda e, src=src, qoff=qoff: e.tensor_mul(
                    src, src, qsb[:, qoff:qoff + 1024]))
                s4 = src.rearrange("p (h u d) -> p h u d", u=2, d=32)
                r4 = lsq[:].rearrange("p (h u d) -> p h u d", u=2, d=32)
                p.op("vector", lambda e, s4=s4, r4=r4: e.tensor_copy(out=r4[:, :, 0, :],
                                                                     in_=s4[:, :, 1, :]))
                p.op("vector", lambda e, s4=s4, r4=r4: e.tensor_copy(out=r4[:, :, 1, :],
                                                                     in_=s4[:, :, 0, :]))
                p.wait("vector", "tl", p.val("tl"))
                p.op("vector", lambda e, src=src: e.tensor_mul(src, src, tabs[:, 0:1024]))
                p.op("vector", lambda e: e.tensor_mul(lsq[:], lsq[:], tabs[:, 1024:2048]))
                p.op("vector", lambda e, src=src: e.tensor_add(src, src, lsq[:]), sem="tu")
                p.wait("tensor", "tu", p.val("tu"))
                if trc > 1:
                    p.wait("tensor", "tev", btev + 8 * (trc - 1))
                for fc in range(8):
                    p.op("tensor", lambda e, src=src, fc=fc: e.transpose(
                        psT[:, 128 * (fc % 8):128 * (fc % 8) + 128],
                        in_=src[:, 128 * fc:128 * (fc + 1)], identity=identb_s[:]),
                        sem="tp")
                p.wait("vector", "tp", p.val("tp"))
                for fc in range(8):
                    if kind == "q":
                        dstT = qT3[:, fc, 128 * ti:128 * (ti + 1)]
                    elif kind == "k":
                        dstT = kT3[:, fc, 256 + 128 * ti:256 + 128 * (ti + 1)]
                    else:
                        dstT = kT3[:, fc, 128 * ti:128 * (ti + 1)]
                    p.op("vector", lambda e, dstT=dstT, fc=fc: e.tensor_copy(
                        out=dstT, in_=psT[:, 128 * fc:128 * (fc + 1)]), sem="tev")
            if hf == 1:
                p.op("vector", lambda e: e.tensor_copy(
                    out=kT3[:, :, 0:256],
                    in_=ksave[:].rearrange("p (c t) -> p c t", t=256)))
            p.vfence("kvrdy")

            if DSTOP == 'rot2' and li == 0 and hf == 0:
                dao = p.nc.dram_tensor("dao", [128, 8 * 1024], BF, kind="ExternalOutput").ap()
                p.op("vector", lambda e: e.engine_nop(), sem="dbgv2")
                p.wait("sync", "aov", p.val("aov") + 128)
                p.op("sync", lambda e: e.dma_start(out=dao, in_=aout), sem="dbg", k=16)
                p.wait("sync", "dbg", p.val("dbg"))
            if DSTOP in ('rot', 'rot2', 'rot3') and li == 0 and hf == 0:
                dq = p.nc.dram_tensor("dq", [128, 8 * 1024], BF, kind="ExternalOutput").ap()
                dk = p.nc.dram_tensor("dk", [128, 10 * 1024], BF, kind="ExternalOutput").ap()
                dv = p.nc.dram_tensor("dv", [128, 10 * 1024], BF, kind="ExternalOutput").ap()
                dxs = p.nc.dram_tensor("dxs", [128, 8 * 1024], BF, kind="ExternalOutput").ap()
                dxh = p.nc.dram_tensor("dxh", [128, 8 * 257], BF, kind="ExternalOutput").ap()
                p.op("vector", lambda e: e.engine_nop(), sem="dbgv")
                p.wait("sync", "dbgv", p.val("dbgv"))
                p.op("sync", lambda e: e.dma_start(out=dq, in_=qT), sem="dbg", k=16)
                p.op("sync", lambda e: e.dma_start(out=dk, in_=kT), sem="dbg", k=16)
                p.op("sync", lambda e: e.dma_start(out=dv, in_=v_tok), sem="dbg", k=16)
                p.op("sync", lambda e: e.dma_start(out=dxs, in_=xs_bf), sem="dbg", k=16)
                p.op("sync", lambda e: e.dma_start(out=dxh, in_=xshal), sem="dbg", k=16)
                p.wait("sync", "dbg", p.val("dbg"))
            p.wait("vector", "tev", p.val("tev"))
            p.vfence("tevf")
            simcnt = 0
            bsimv, baov = p.val("simv"), p.val("aov")
            bpex, bptp, bive = p.val("pex"), p.val("ptp"), p.val("ive")
            for w in range(4):
                for h in range(16):
                    fc, pb0 = h // 2, 64 * (h % 2)
                    for qs_i in range(2):
                        qcol = 256 * w + 128 * qs_i
                        mk = maskA_s
                        simb = 2 + (simcnt % 2)
                        simcnt += 1
                        p.wait("tensor", "kvrdy", p.val("kvrdy"))
                        p.wait("tensor", "tevf", p.val("tevf"))
                        if simcnt > 2:
                            p.wait("tensor", "simv", bsimv + simcnt - 2)
                        if False:
                            p.op("tensor", lambda e: e.drain())
                        p.op("tensor", lambda e, fc=fc, pb0=pb0, qcol=qcol, simb=simb, w=w: e.matmul(
                            psA[:, 512 * simb:512 * (simb + 1)],
                            lhsT=qT3[pb0:pb0 + 64, fc, qcol:qcol + 128],
                            rhs=kT3[pb0:pb0 + 64, fc, 256 * w:256 * w + 512],
                            start=True, stop=True), sem="simm")
                        if False:
                            p.op("tensor", lambda e: e.drain())
                        p.wait("vector", "simm", p.val("simm"))
                        if simcnt > 1:
                            p.wait("vector", "pex", bpex + simcnt - 1)
                        p.op("vector", lambda e, mk=mk, qs_i=qs_i, simb=simb: e.tensor_add(
                            simc[:], psA[:, 512 * simb:512 * (simb + 1)],
                            mk[:, 512 * qs_i:512 * (qs_i + 1)]),
                            sem=(None if (hf == 0 and w == 0) else "simv"))
                        if hf == 0 and w == 0:
                            p.op("vector", lambda e: e.tensor_scalar_add(
                                simc[:, 0:256], simc[:, 0:256], mkneg_s[:]), sem="simv")
                        if DSTOP == 'rot3' and li == 0 and hf == 0 and h == 0 and qs_i == 0:
                            dsim = p.nc.dram_tensor(f"dsim{w}", [128, 512], F32,
                                                    kind="ExternalOutput").ap()
                            p.op("vector", lambda e: e.engine_nop(), sem="dsf")
                            p.wait("sync", "dsf", p.val("dsf"))
                            p.op("sync", lambda e, dsim=dsim: e.dma_start(out=dsim, in_=simc[:]),
                                 sem="dbg", k=16)
                            p.wait("vector", "dbg", p.val("dbg"))
                        p.op("vector", lambda e: e.reduce_max(nmx[:], simc[:], axis=AX.X,
                                                              negate=True))
                        p.self_edge("vector", "mxe")
                        p.wait("scalar", "mxe", p.val("mxe"))
                        if simcnt > 1:
                            p.wait("scalar", "ptp", bptp + (simcnt - 1))
                            p.wait("scalar", "ive", bive + simcnt - 1)
                        p.op("scalar", lambda e: e.activation(out=pexp[:], in_=simc[:],
                                                              func=AF.Exp, bias=nmx[:],
                                                              accum_out=rsum[:]), sem="pex")
                        p.wait("vector", "pex", p.val("pex"))
                        p.op("vector", lambda e: e.reciprocal(invz[:], rsum[:]))
                        p.self_edge("vector", "ive")
                        p.op("vector", lambda e: e.tensor_scalar_mul(pexp[:], pexp[:], invz[:]))
                        p.vfence("psc")
                        p.wait("tensor", "psc", p.val("psc"))
                        for jc in range(4):
                            p.op("tensor", lambda e, jc=jc: e.transpose(
                                psT[:, 128 * jc:128 * (jc + 1)],
                                in_=pexp[:, 128 * jc:128 * (jc + 1)], identity=identb_s[:]))
                        p.mark("tensor", "ptp")
                        p.wait("vector", "ptp", p.val("ptp"))
                        p.op("vector", lambda e: e.tensor_copy(out=ptb[:], in_=psT[:, 0:512]))
                        p.vfence("ptv")
                        p.wait("tensor", "ptv", p.val("ptv"))
                        if simcnt > 1:
                            p.wait("tensor", "aov", baov + simcnt - 1)
                        for jc in range(4):
                            vt = 2 * hf * 4 + 2 * w + jc
                        for jc in range(4):
                            vtile = 2 * w + jc
                            p.op("tensor", lambda e, jc=jc, vtile=vtile, h=h: e.matmul(
                                psB[0:64, 1024:1152], lhsT=vv3[:, vtile, 64 * h:64 * (h + 1)],
                                rhs=ptb[:, 128 * jc:128 * (jc + 1)],
                                start=(jc == 0), stop=(jc == 3)))
                        p.mark("tensor", "pvm")
                        p.wait("vector", "pvm", p.val("pvm"))
                        p.op("vector", lambda e, fc=fc, pb0=pb0, qcol=qcol: e.tensor_copy(
                            out=ao3[pb0:pb0 + 64, fc, qcol:qcol + 128],
                            in_=psB[0:64, 1024:1152]), sem="aov")
            p.wait("vector", "aov", p.val("aov"))
            p.vfence("aovf")
            p.wait("sync", "wcast", p.val("wcast"))
            bwoc, bwol, bwom, bwov = (p.val("woc"), p.val("wol"), p.val("wom"), p.val("wov"))
            bwocf = p.val("wocf")
            for fo in range(8):
                pp = fo % 2
                if fo >= 1:
                    p.wait("sync", "woc", bwoc + fo)
                p.op("sync", lambda e, fo=fo, li=li: e.dma_start(
                    out=wst[:, 0:1024].rearrange("p (c w) -> p c w", w=128),
                    in_=io["Wo"].rearrange("l (c p) w -> p l c w", p=128)[:, li, :, 128 * fo:128 * (fo + 1)]),
                    sem="wol", k=16)
                p.wait("vector", "wol", bwol + 16 * (fo + 1))
                if fo == 0:
                    p.wait("vector", "pje", p.val("pje"))
                elif fo >= 2:
                    p.wait("vector", "wom", bwom + 2 * (fo - 1))
                p.op("vector", lambda e, pp=pp: e.tensor_copy(
                    out=wbb[:, pp * 2048:pp * 2048 + 1024], in_=wst[:, 0:1024]), sem="woc")
                p.vfence("wocf")
                p.wait("tensor", "wocf", bwocf + fo + 1)
                if fo == 0:
                    p.wait("tensor", "aovf", p.val("aovf"))
                if fo >= 1:
                    p.wait("tensor", "wov", bwov + fo)
                for n in range(2):
                    for k in range(8):
                        p.op("tensor", lambda e, fo=fo, pp=pp, n=n, k=k: e.matmul(
                            psB[:, 512 * n:512 * (n + 1)],
                            lhsT=wbb[:, pp * 2048 + k * 128:pp * 2048 + k * 128 + 128],
                            rhs=ao3[:, k, 512 * n:512 * (n + 1)],
                            start=(k == 0), stop=(k == 7)))
                    p.mark("tensor", "wom")
                p.wait("vector", "wom", bwom + 2 * (fo + 1))
                p.op("vector", lambda e, fo=fo, t0=t0: e.tensor_add(
                    xf[:, fo * TP + t0:fo * TP + t0 + TH],
                    xf[:, fo * TP + t0:fo * TP + t0 + TH], psB[:, 0:1024]), sem="wov")

        p.wait("vector", "wov", p.val("wov"))
        p.op("vector", lambda e: e.tensor_copy(out=tmp[:, 0:8].rearrange("p (a b) -> p a b", b=1),
                                               in_=xf3[:, :, TP - 1:TP]), sem="hc")
        p.wait("sync", "hc", p.val("hc"))
        p.op("sync", lambda e, agb=agb: e.dma_start(out=io[f"hB_in{agb}"], in_=tmp[:, 0:8]),
             sem="hst", k=16)
        p.wait("gpsimd", "hst", p.val("hst"))
        p.op("gpsimd", lambda e, agb=agb: e.collective_compute(
            "AllGather", ALU.bypass, replica_groups=[list(range(8))],
            ins=[io[f"hB_in{agb}"].opt()], outs=[io[f"hB_out{agb}"].opt()]), sem="ccA")
        p.wait("gpsimd", "ccA", p.val("ccA"))
        p.op("gpsimd", lambda e, agb=agb: e.indirect_dma_start(
            out=tmp[:, 8:16], out_offset=None, in_=io[f"hB_out{agb}"],
            in_offset=bass.IndirectOffsetOnAxis(ap=idx_s[:], axis=0)), sem="hgot", k=16)
        p.wait("vector", "hgot", p.val("hgot"))
        p.op("vector", lambda e: e.tensor_scalar_mul(tmp[:, 8:16], tmp[:, 8:16], hz_s[:]))
        p.op("vector", lambda e: e.tensor_copy(
            out=xf3[:, :, 0:1], in_=tmp[:, 8:16].rearrange("p (c o) -> p c o", o=1)),
            sem="hrdy")

        if DSTOP == 'attn':
            break
        for hf in range(2):
            t0 = 1 + TH * hf
            H = f"{L}f{hf}"
            p.wait("vector", "hrdy", p.val("hrdy"))
            if hf == 1:
                p.wait("vector", "w2mm", p.val("w2mm"))
            for c in range(8):
                off = t0 if c < 4 else t0 - 1
                p.op("vector", lambda e, c=c, off=off: e.tensor_copy(
                    out=xs2v[:, c, :], in_=xf[:, c * TP + off:c * TP + off + TH]), sem="xs2")
            p.wait("vector", "xs2", p.val("xs2"))
            emit_ln(p, "fln", lambda c: xs2v[:, c, :],
                    lambda c: ln3[:, c, :], tmp[:, 0:TH], xsq2,
                    psA[:, 0:1024], psB[:, 0:1024], onec_s[:], oner_s[:], TH,
                    rows=rows, eps_ap=cvec_s[0:1, 0:1],
                    gamma_pp=lambda c: lng_s[:, 8 * li + c:8 * li + c + 1],
                    beta_pp=lambda c: lnb_s[:, 8 * li + c:8 * li + c + 1])
            p.wait("vector", "fln_done", p.val("fln_done"))
            p.vfence("flnf")
            p.wait("tensor", "flnf", p.val("flnf"))
            p.wait("sync", "flnf", p.val("flnf"))
            bw1l, bw1c, bw1mm = p.val("w1l"), p.val("w1c"), p.val("w1mm")
            bw1f = p.val("w1f")
            bgel, bh2v = p.val("gel"), p.val("h2v")
            for j in range(NMT):
                a0, a1 = 128 * j, min(DI, 128 * (j + 1))
                g0 = DI + 128 * j
                g1 = min(2 * DI, DI + 128 * (j + 1))
                aw, gw = a1 - a0, g1 - g0
                pp = j % 2
                if j >= 1:
                    p.wait("sync", "w1c", bw1c + 2 * j)
                p.op("sync", lambda e, g0=g0, gw=gw, li=li: e.dma_start(
                    out=wst[:, 0:8 * gw].rearrange("p (c w) -> p c w", w=gw),
                    in_=io["W1"].rearrange("l (c p) w -> p l c w", p=128)[:, li, :, g0:g0 + gw]),
                    sem="w1l", k=16)
                p.op("sync", lambda e, a0=a0, aw=aw, li=li: e.dma_start(
                    out=wst[:, 1024:1024 + 8 * aw].rearrange("p (c w) -> p c w", w=aw),
                    in_=io["W1"].rearrange("l (c p) w -> p l c w", p=128)[:, li, :, a0:a0 + aw]),
                    sem="w1l", k=16)
                p.wait("vector", "w1l", bw1l + 32 * (j + 1))
                if j >= 2:
                    p.wait("vector", "w1mm", bw1mm + 4 * (j - 1))
                p.op("vector", lambda e, pp=pp, gw=gw: e.tensor_copy(
                    out=wbb[:, pp * 2048:pp * 2048 + 8 * gw], in_=wst[:, 0:8 * gw]), sem="w1c")
                p.op("vector", lambda e, pp=pp, aw=aw: e.tensor_copy(
                    out=wbb[:, pp * 2048 + 1024:pp * 2048 + 1024 + 8 * aw],
                    in_=wst[:, 1024:1024 + 8 * aw]), sem="w1c")
                p.vfence("w1f")
                p.wait("tensor", "w1f", bw1f + j + 1)
                if j >= 1:
                    p.wait("tensor", "h2v", bh2v + j)
                    p.wait("tensor", "gel", bgel + j)
                for n in range(2):
                    for k in range(8):
                        p.op("tensor", lambda e, pp=pp, k=k, n=n, gw=gw: e.matmul(
                            psA[0:gw, 512 * n:512 * (n + 1)],
                            lhsT=wbb[:, pp * 2048 + k * gw:pp * 2048 + (k + 1) * gw],
                            rhs=ln3[:, k, 512 * n:512 * (n + 1)],
                            start=(k == 0), stop=(k == 7)))
                    p.mark("tensor", "w1mm")
                p.wait("scalar", "w1mm", bw1mm + 4 * j + 2)
                if j >= 1:
                    p.wait("scalar", "h2v", bh2v + j)
                p.op("scalar", lambda e, j=j, li=li: e.activation(
                    out=gbuf[:], in_=psA[:, 0:1024], func=AF.Gelu,
                    bias=b1g_s[:, NMT * li + j:NMT * li + j + 1]), sem="gel")
                for n in range(2):
                    for k in range(8):
                        p.op("tensor", lambda e, pp=pp, k=k, n=n, aw=aw: e.matmul(
                            psA[0:aw, 1024 + 512 * n:1024 + 512 * (n + 1)],
                            lhsT=wbb[:, pp * 2048 + 1024 + k * aw:pp * 2048 + 1024 + (k + 1) * aw],
                            rhs=ln3[:, k, 512 * n:512 * (n + 1)],
                            start=(k == 0), stop=(k == 7)))
                    p.mark("tensor", "w1mm")
                p.wait("vector", "w1mm", bw1mm + 4 * (j + 1))
                p.wait("vector", "gel", bgel + j + 1)
                p.op("vector", lambda e, j=j, li=li: e.scalar_tensor_tensor(
                    out=h23[:, j, :], in0=psA[:, 1024:2048],
                    scalar=b1a_s[:, NMT * li + j:NMT * li + j + 1], in1=gbuf[:],
                    op0=ALU.add, op1=ALU.mult), sem="h2v")
            if DSTOP == 'ffn1' and li == 0 and hf == 0:
                dln = p.nc.dram_tensor("dln", [128, 8 * 1024], BF, kind="ExternalOutput").ap()
                dh2 = p.nc.dram_tensor("dh2", [128, NMT * 1024], BF, kind="ExternalOutput").ap()
                dxs2 = p.nc.dram_tensor("dxs2", [128, 8 * 1024], BF, kind="ExternalOutput").ap()
                p.op("vector", lambda e: e.engine_nop(), sem="dbgv3")
                p.wait("sync", "dbgv3", p.val("dbgv3"))
                p.op("sync", lambda e: e.dma_start(out=dln, in_=ln_ff), sem="dbg", k=16)
                p.op("sync", lambda e: e.dma_start(out=dh2, in_=h2), sem="dbg", k=16)
                p.op("sync", lambda e: e.dma_start(out=dxs2, in_=xs2_bf), sem="dbg", k=16)
                p.wait("sync", "dbg", p.val("dbg"))
            p.wait("sync", "w1c", bw1c + 2 * NMT)
            p.wait("vector", "h2v", p.val("h2v"))
            p.vfence("h2f")
            KSPL = ((0, 11), (11, NMT))
            bw2c, bw2mm, bffev = p.val("w2c"), p.val("w2mm"), p.val("ffev")
            bw2f = p.val("w2f")
            for fo in range(8):
                for half_i, (kk0, kk1) in enumerate(KSPL):
                    pp = (2 * fo + half_i) % 2
                    rr0 = 128 * kk0
                    rr1 = min(DI, 128 * kk1)
                    nr = rr1 - rr0
                    idx2 = 2 * fo + half_i
                    if idx2 >= 1:
                        p.wait("sync", "w2c", bw2c + idx2)
                    nfull = (nr // 128) * 128
                    nch = nr // 128
                    p.op("sync", lambda e, rr0=rr0, nfull=nfull, nch=nch, fo=fo, li=li: e.dma_start(
                        out=wst[:, 0:nfull].rearrange("p (c w) -> p c w", w=128),
                        in_=io["W2"][li, rr0:rr0 + nfull, 128 * fo:128 * (fo + 1)].rearrange(
                            "(c p) w -> p c w", p=128)), sem="w2l", k=16)
                    ndma = 1
                    if nr % 128:
                        p.op("sync", lambda e, rr0=rr0, nfull=nfull, nr=nr, nch=nch, fo=fo, li=li: e.dma_start(
                            out=wst[0:nr - nfull, nfull:nfull + 128],
                            in_=io["W2"][li, rr0 + nfull:rr0 + nr, 128 * fo:128 * (fo + 1)]),
                            sem="w2l", k=16)
                        ndma = 2
                    w2l_cum = p.val("w2l")
                    p.wait("vector", "w2l", w2l_cum)
                    if idx2 >= 2:
                        p.wait("vector", "w2mm", bw2mm + 2 * (idx2 - 1))
                    p.op("vector", lambda e, pp=pp, nr=nr: e.tensor_copy(
                        out=wbb[:, pp * 2048:pp * 2048 + nr], in_=wst[:, 0:nr]), sem="w2c")
                    p.vfence("w2f")
                    p.wait("tensor", "w2f", bw2f + idx2 + 1)
                    if fo == 0 and half_i == 0:
                        p.wait("tensor", "h2f", p.val("h2f"))
                    if fo >= 1 and half_i == 0:
                        p.wait("tensor", "ffev", bffev + fo)
                    for n in range(2):
                        for kk in range(kk0, kk1):
                            kw = 128 if kk < NMT - 1 else LASTR
                            koff = 128 * (kk - kk0)
                            p.op("tensor", lambda e, pp=pp, kk=kk, kw=kw, koff=koff, n=n: e.matmul(
                                psB[:, 512 * n:512 * (n + 1)],
                                lhsT=wbb[0:kw, pp * 2048 + koff:pp * 2048 + koff + 128],
                                rhs=h23[0:kw, kk, 512 * n:512 * (n + 1)],
                                start=(kk == 0), stop=(kk == NMT - 1)))
                        p.mark("tensor", "w2mm")
                p.wait("vector", "w2mm", bw2mm + 4 * (fo + 1))
                p.op("vector", lambda e, fo=fo, t0=t0, li=li: e.scalar_tensor_tensor(
                    out=xf[:, fo * TP + t0:fo * TP + t0 + TH],
                    in0=psB[:, 0:1024],
                    scalar=b2_s[:, 8 * li + fo:8 * li + fo + 1],
                    in1=xf[:, fo * TP + t0:fo * TP + t0 + TH],
                    op0=ALU.add, op1=ALU.add), sem="ffev")
        p.op("vector", lambda e: e.engine_nop(), sem="ffx")

    if DSTOP == 'attn':
        p.wait("sync", "wov", p.val("wov"))
    if DSTOP == 'ln4':
        p.wait("sync", "ln4c", p.val("ln4c"))
    p.wait("vector", "ffx", p.val("ffx"))
    p.wait("sync", "w2c", p.val("w2c"))
    for c in range(8):
        if not DSTOP:
            p.op("sync", lambda e, c=c: e.dma_start(out=wst[:], in_=xin3[:, c, :]),
                 sem="fx", k=16)
            p.wait("vector", "fx", 16 * (c + 1))
            p.op("vector", lambda e, c=c: e.tensor_add(
                xf[:, c * TP + 1:c * TP + 1 + T], xf[:, c * TP + 1:c * TP + 1 + T],
                wst[:]), sem="fadd")
            p.wait("sync", "fadd", c + 1)
        p.op("sync", lambda e, c=c: e.dma_start(
            out=xout3[:, c, :], in_=xf[:, c * TP + 1:c * TP + 1 + T]), sem="stx", k=16)
    p.wait("sync", "stx", 16 * 8)


def host_tables():
    DIM_HEAD, WINDOW = 64, 256
    half = np.arange(0, DIM_HEAD, 2, dtype=np.float64)
    inv_freq = 1.0 / 10000.0 ** (half / DIM_HEAD)
    n = 16384
    t = np.arange(n, dtype=np.float64)
    freqs = t[:, None] * inv_freq[None, :]
    freqs = np.concatenate([freqs, freqs], -1)
    base_scale = (half + 0.4 * DIM_HEAD) / (1.4 * DIM_HEAD)
    power = (t - n // 2) / (WINDOW // 2)
    scale = base_scale[None, :] ** power[:, None]
    scale = np.concatenate([scale, scale], -1)
    cos, sin = np.cos(freqs), np.sin(freqs)
    CQ = cos * scale
    SQ = sin * scale
    CK = cos / scale
    SK = sin / scale
    sgn = np.concatenate([-np.ones(32), np.ones(32)])
    SQ2 = SQ * sgn[None, :]
    SK2 = SK * sgn[None, :]
    return CQ, SQ2, CK, SK2


def core_inputs_d(z, x3, c):
    CQ, SQ2, CK, SK2 = host_tables()

    def tab(C, S):
        pos = np.arange(2048 * c - 256, 2048 * c + 2048)
        ok = pos >= 0
        posc = np.clip(pos, 0, None)
        Cc = np.where(ok[:, None], C[posc], 0.0)
        Sc = np.where(ok[:, None], S[posc], 0.0)
        out = np.concatenate([np.tile(Cc, (1, 16)), np.tile(Sc, (1, 16))], axis=1)
        return out.astype(np.float32)

    i = np.arange(256)[:, None]
    jj = np.arange(512)[None, :]
    jb = np.where(jj < 256, jj - 256, jj - 256)
    rel = np.where(jj < 256, jj - 256, jj - 256)
    d = i - (jj - 256)
    allowed = (d >= 0) & (d <= 256)
    mA = np.where(allowed, 0.0, -1e30).astype(np.float32)
    packm = lambda m: np.concatenate([m[0:128], m[128:256]], axis=1).astype(ml_dtypes.bfloat16)
    idxprev = (((c - 1) % 8) * 128 + np.arange(128)).astype(np.uint32).reshape(128, 1)
    cv = np.zeros((1, 8), np.float32)
    cv[0, 0] = 1e-5
    cv[0, 2] = 1e-30
    fs = lambda a: np.ascontiguousarray(a.astype(np.float32))
    d_ = dict(
        x3=np.ascontiguousarray(x3[2048 * c:2048 * (c + 1)].T),
        Wqkv=z["L1_ffn_Wqkv"], Wo=z["L1_ffn_Wo"], W1=z["L1_ffn_W1"], W2=z["L1_ffn_W2"],
        qst=fs(np.tile(z["L1_ffn_q_scale"], (1, 16))),
        kst=fs(np.tile(z["L1_ffn_k_scale"], (1, 16))),
        lng=fs(z["L1_ffn_ln_g"].reshape(NL, 8, 128).transpose(2, 0, 1).reshape(128, NL * 8)[:, :]),
        lnb=fs(z["L1_ffn_ln_b"].reshape(NL, 8, 128).transpose(2, 0, 1).reshape(128, NL * 8)),
        b2=fs(z["L1_ffn_b2"].reshape(NL, 8, 128).transpose(2, 0, 1).reshape(128, NL * 8)),
        tabq=tab(CQ, SQ2), tabk=tab(CK, SK2),
        maskA=packm(mA),
        mknegv=np.full((128, 1), -1e30 if c == 0 else 0.0, np.float32),
        idxprev=idxprev, hz=np.full((128, 1), 0.0 if c == 0 else 1.0, np.float32),
        eps30=np.full((128, 1), 1e-30, np.float32),
        identb=np.eye(128, dtype=ml_dtypes.bfloat16),
        onec=np.ones((128, 1), ml_dtypes.bfloat16),
        oner=np.ones((1, 128), np.float32), cvec=cv)
    b1 = z["L1_ffn_b1"]
    b1a = np.zeros((128, NMT * NL), np.float32)
    b1g = np.zeros((128, NMT * NL), np.float32)
    for li in range(NL):
        for j in range(NMT):
            a0, a1 = 128 * j, min(DI, 128 * (j + 1))
            b1a[0:a1 - a0, NMT * li + j] = b1[li, a0:a1]
            g0, g1 = DI + 128 * j, min(2 * DI, DI + 128 * (j + 1))
            b1g[0:g1 - g0, NMT * li + j] = b1[li, g0:g1]
    d_["b1a"] = b1a
    d_["b1g"] = b1g
    lng = np.zeros((128, 8 * NL), np.float32)
    lnb = np.zeros((128, 8 * NL), np.float32)
    b2m = np.zeros((128, 8 * NL), np.float32)
    for li in range(NL):
        for ch in range(8):
            lng[:, 8 * li + ch] = z["L1_ffn_ln_g"][li, 128 * ch:128 * (ch + 1)]
            lnb[:, 8 * li + ch] = z["L1_ffn_ln_b"][li, 128 * ch:128 * (ch + 1)]
            b2m[:, 8 * li + ch] = z["L1_ffn_b2"][li, 128 * ch:128 * (ch + 1)]
    d_["lng"], d_["lnb"], d_["b2"] = lng, lnb, b2m
    return d_





def kernel(**inputs):
    z = inputs
    x0 = np.ascontiguousarray(np.asarray(z["L0_attn_x"], np.float32).reshape(16384, 1024))
    onec = np.ones((128, 1), ml_dtypes.bfloat16)
    oner = np.ones((1, 128), np.float32)
    cv = np.zeros((1, 8), np.float32)
    cv[0, 0] = 1e-5

    nc = build_a()
    in_maps = [dict(xin=np.ascontiguousarray(x0[2048 * c:2048 * (c + 1)].T),
                    Mq=z["L0_attn_Mq"], Mk=z["L0_attn_Mk"], Mv=z["L0_attn_Mv"],
                    onec=onec, oner=oner, cvec=cv) for c in range(8)]
    res = run_bass_kernel_spmd(nc, in_maps, core_ids=list(range(8)))
    x1 = np.concatenate([np.asarray(r["xout"], np.float32).T for r in res.results], axis=0)
    del res, in_maps

    nc = build_b()
    wgT = np.ascontiguousarray(z["L0_ffn_w_gate"].T)
    wuT = np.ascontiguousarray(z["L0_ffn_w_up"].T)
    wdT = np.ascontiguousarray(z["L0_ffn_w_down"].T)
    in_maps = [dict(xin=np.ascontiguousarray(x1[2048 * c:2048 * (c + 1)].T),
                    wgT=wgT, wuT=wuT, wdT=wdT, onec=onec, oner=oner, cvec=cv)
               for c in range(8)]
    res = run_bass_kernel_spmd(nc, in_maps, core_ids=list(range(8)))
    x2 = np.concatenate([np.asarray(r["xout"], np.float32).T for r in res.results], axis=0)
    del res, in_maps

    nc = build_c()
    in_maps = [core_inputs_c(z, x2, h) for h in range(8)]
    res = run_bass_kernel_spmd(nc, in_maps, core_ids=list(range(8)))
    x3 = np.concatenate([np.asarray(r["xout"], np.float32).T for r in res.results], axis=0)
    del res, in_maps

    nc = build_d()
    in_maps = [core_inputs_d(z, x3, c) for c in range(8)]
    res = run_bass_kernel_spmd(nc, in_maps, core_ids=list(range(8)))
    x4 = np.concatenate([np.asarray(r["xout"], np.float32).T for r in res.results], axis=0)
    del res, in_maps

    return np.ascontiguousarray(x4.astype(np.float32).reshape(1, 16384, 1024))
